# version 01
# baseline (speedup 1.0000x reference)
import bisect
import contextlib
import os

import numpy as np
import ml_dtypes

import concourse.bass as bass
import concourse.mybir as mybir
from concourse.bass_utils import run_bass_kernel_spmd

F32 = mybir.dt.float32
BF16 = mybir.dt.bfloat16
AF = mybir.ActivationFunctionType
ALU = mybir.AluOpType
AX = mybir.AxisListType.X

N_CORES = 8
D = 1024
T = 2048
ND = D // 128
NT = T // 128


ENGINES = ("tensor", "vector", "scalar", "gpsimd", "sync")
DMA_QUEUES = {"sync": "q_sync", "scalar": "q_scalar", "gpsimd": "q_gp"}


def _keys(aps, resolver=None):
    out = []
    if aps is None:
        return out
    if not isinstance(aps, (list, tuple)):
        aps = [aps]
    for a in aps:
        if a is None:
            continue
        if isinstance(a, str):
            out.append(a)
            continue
        t = getattr(a, "tensor", None)
        name = getattr(t, "name", None) or getattr(a, "name", None)
        name = str(name)
        if resolver is not None and t is not None:
            name = resolver(a)
        out.append(name)
    return out


class _Op:
    __slots__ = ("engine", "emit", "seq", "waits", "dma", "tag")

    def __init__(self, engine, emit, seq, dma, tag):
        self.engine = engine
        self.emit = emit
        self.seq = seq
        self.dma = dma
        self.tag = tag
        self.waits = []


class Prog:
    GROUP_CAP = 16

    def __init__(self, nc):
        self.nc = nc
        self.plan = {e: [] for e in ENGINES}
        self.counts = dict.fromkeys(ENGINES, 0)
        self.qcounts = dict.fromkeys(DMA_QUEUES.values(), 0)
        self.writer = {}
        self.readers = {}
        self.extra_deps = {}
        self.resolver = None
        self.n_ops = 0
        self.n_waits = 0
        self.qgroup_open = {q: [] for q in DMA_QUEUES.values()}
        self.qgroup_end = {q: {} for q in DMA_QUEUES.values()}
        self.qgroup_prev_end = dict.fromkeys(DMA_QUEUES.values(), 0)

    def _close_group(self, q):
        mem = self.qgroup_open[q]
        if not mem:
            return
        end = mem[-1]
        for s in mem:
            self.qgroup_end[q][s] = end
        self.qgroup_prev_end[q] = end
        self.qgroup_open[q] = []

    def op(self, engine, emit, reads=(), writes=(), dma=False, tag=None):
        if (getattr(self, "limit", None) is not None
                and self.n_ops >= self.limit
                and tag not in ("ld_x", "st_x", "flush", "pdbar")
                and not (tag or "").startswith("pd_")):
            o = _Op(engine, emit, 0, dma, tag)
            return o
        rk = _keys(reads, self.resolver)
        wk = _keys(writes, self.resolver)
        if dma:
            q = DMA_QUEUES[engine]
            if len(self.qgroup_open[q]) >= self.GROUP_CAP:
                self._close_group(q)
            self.qcounts[q] += 16
            src, seq = q, self.qcounts[q]
            new_group = not self.qgroup_open[q]
            self.qgroup_open[q].append(seq)
        else:
            self.counts[engine] += 1
            src, seq = engine, self.counts[engine]
        o = _Op(engine, emit, seq, dma, tag)
        waits = {}

        def need(dep):
            if dep is None:
                return
            dsrc, dseq = dep
            if dsrc == "tensor" and engine == "tensor" and not dma:
                return
            if dsrc in self.qcounts:
                if dma and dsrc == src and dseq in self.qgroup_open[dsrc]:
                    return
                if dseq in self.qgroup_open[dsrc]:
                    self._close_group(dsrc)
                dseq = self.qgroup_end[dsrc].get(dseq, dseq)
            waits[dsrc] = max(waits.get(dsrc, 0), dseq)

        for k in rk:
            need(self.writer.get(k))
        for k in wk:
            need(self.writer.get(k))
            for dep in self.readers.get(k, ()):
                need(dep)
        for k in set(rk) | set(wk):
            for dep in self.extra_deps.pop(k, ()):
                need(dep)
        if dma and new_group and self.qgroup_prev_end[src] > 0:
            waits[src] = max(waits.get(src, 0), self.qgroup_prev_end[src])
        o.waits = sorted(waits.items())
        dr = getattr(self, "drain_tags", None)
        if dr and tag in dr and self.counts["tensor"] > 0:
            o.waits = sorted(dict(o.waits + [("tensor",
                                              self.counts["tensor"])]).items())
        self.n_waits += len(o.waits)
        for k in wk:
            self.writer[k] = (src, seq)
            self.readers[k] = []
        for k in rk:
            self.readers.setdefault(k, []).append((src, seq))
        self.plan[engine].append(o)
        self.n_ops += 1
        return o

    def inherit(self, new_key, old_keys):
        deps = []
        for k in old_keys:
            w = self.writer.get(k)
            if w:
                deps.append(w)
            deps.extend(self.readers.get(k, ()))
        if deps:
            self.extra_deps.setdefault(new_key, []).extend(deps)

    def barrier(self, tag="barrier"):
        waits = [(e, self.counts[e]) for e in ENGINES if self.counts[e] > 0]
        waits += [(q, c) for q, c in self.qcounts.items() if c > 0]
        for e in ENGINES:
            o = self.op(e, lambda eng: eng.nop(), [], [], tag=tag)
            extra = [(s, c) for (s, c) in waits if s != e]
            o.waits = extra
            self.n_waits += len(extra)

    def t(self, emit, reads=(), writes=(), tag=None):
        return self.op("tensor", emit, reads, writes, tag=tag)

    def v(self, emit, reads=(), writes=(), tag=None):
        return self.op("vector", emit, reads, writes, tag=tag)

    def s(self, emit, reads=(), writes=(), tag=None):
        return self.op("scalar", emit, reads, writes, tag=tag)

    def g(self, emit, reads=(), writes=(), tag=None):
        return self.op("gpsimd", emit, reads, writes, tag=tag)

    def dma(self, queue, out, in_, extra_reads=(), tag=None):
        def emit(eng, out=out, in_=in_):
            return eng.dma_start(out=out, in_=in_)
        return self.op(queue, emit, [in_] + list(extra_reads), [out], dma=True,
                       tag=tag)

    def emit_all(self, block, sems, qsems):
        targets = {}
        for e in ENGINES:
            for o in self.plan[e]:
                for (src, seq) in o.waits:
                    targets.setdefault(src, set()).add(seq)
        inc_rank = {}
        for e in ENGINES:
            ranks = {}
            r = 0
            for o in self.plan[e]:
                if o.dma:
                    continue
                if o.seq in targets.get(e, ()):
                    r += 1
                    ranks[o.seq] = r
            inc_rank[e] = ranks
        sorted_incs = {e: sorted(inc_rank[e]) for e in ENGINES}

        def wait_value(src, seq):
            if src in self.qcounts:
                return seq
            lst = sorted_incs[src]
            i = bisect.bisect_left(lst, seq)
            return inc_rank[src][lst[i]]

        prog = self

        def runner(ename):
            def run(eng):
                watermark = {}
                for o in prog.plan[ename]:
                    for (src, seq) in o.waits:
                        val = wait_value(src, seq)
                        if watermark.get(src, 0) >= val:
                            continue
                        sem = qsems[src] if src in prog.qcounts else sems[src]
                        eng.wait_ge(sem, val)
                        watermark[src] = val
                    ins = o.emit(eng)
                    if o.dma:
                        ins.then_inc(qsems[DMA_QUEUES[ename]], 16)
                    elif o.seq in inc_rank[ename]:
                        ins.then_inc(sems[ename], 1)
            return run

        for ename in ENGINES:
            if self.plan[ename]:
                getattr(block, ename)(runner(ename))


BLOCK_LOCAL_POOLS = frozenset((
    "rbc_sb",
    "perm_tm", "nmr_tm", "l0f_biasT", "glmask", "l0f_r01tmp",
    *[f"l0f_inner{i}" for i in range(16)],
    "l1a_convw", "l1a_convb", "l1a_dtb", "l1a_alog", "l1a_dpar",
    "l1a_normw", "l1a_tri0", "l1a_tri1",
    "l1a_negA", "l1a_dbc", "l1a_diagD", "l1a_S", "l1a_ccarry", "l1a_sbase",
    "l1a_cchunk", "l1a_slast", "l1a_sdecay", "l1a_sdbc", "l1a_srow16",
    "l1a_einit_tm", "l1a_w_tm", "l1a_dt_tm", "l1a_negs_tm", "l1a_s_tm",
    "l1a_bdtm", "l1a_ysc", "l1a_halo", "rbc_sb2",
    "l1f_qs", "l1f_lng", "l1f_lnb", "l1f_lnbb", "l1f_lngb", "l1f_b2",
    "l1f_rotp", "l1f_triA", "l1f_triB", "l1f_b1", "l1f_bo64", "l1f_repl2",
    "l1f_lbrow", "lnr_tm",
))


class Ctx:
    def __init__(self, nc, prog, stack):
        self.nc = nc
        self.p = prog
        self.stack = stack
        self.c = {}
        self._uid = 0

    def uid(self, base):
        self._uid += 1
        return f"{base}{self._uid}"

    def sb(self, name, shape, dtype):
        return self.stack.enter_context(self.nc.sbuf_tensor(name, shape, dtype))

    def ps(self, name, shape, dtype=F32):
        return self.stack.enter_context(self.nc.psum_tensor(name, shape, dtype))

    def pool(self, name, shape, dtype):
        if name in self.c:
            t = self.c[name]
            assert list(t.shape) == list(shape) and t.dtype == dtype, \
                (name, t.shape, shape, t.dtype, dtype)
            return t
        if name in BLOCK_LOCAL_POOLS and "bscr" in self.c.get("_arenas", {}):
            isz = 2 if dtype == BF16 else 4
            nb = ((shape[1] * isz + 31) // 32) * 32
            ptr = getattr(self, "_bptr", 0)
            full = self.carve("bscr", name, 0, shape[0], ptr, nb, dtype)
            self._bptr = ptr + nb
            t = full[:, 0:shape[1]]
            self.c[name] = t
            return t
        t = self.sb(name, shape, dtype)
        self.c[name] = t
        return t

    def bpool_reset(self):
        arenas = self.c.get("_arenas", {})
        a = arenas.get("bscr")
        if a is not None:
            for n in list(a["regions"]):
                a["regions"].pop(n)
                self.c.pop(n, None)
            self._bptr = 0
        for sn in "ABC":
            s = arenas.get(f"set{sn}")
            if s is not None:
                s["regions"].clear()
                self.c.pop(f"_tiles_{sn}", None)
        m = arenas.get("arena")
        if m is not None:
            for n in list(m["regions"]):
                if n not in ("x0", "x1", "x2"):
                    m["regions"].pop(n)

    def pool_tiles(self, base, n, shape, dtype):
        return [self.pool(f"{base}{i}", shape, dtype) for i in range(n)]

    def arena_init(self, aname, nbytes):
        ar = self.c.setdefault("_arenas", {})
        ar[aname] = {
            "t": self.sb(aname, [128, nbytes // 2], BF16),
            "regions": {}, "hist": [], "rowbytes": nbytes,
        }
        self.p.resolver = self._resolve_arena

    def _resolve_arena(self, ap):
        t = getattr(ap, "tensor", None)
        name = str(getattr(t, "name", None) or getattr(ap, "name", None))
        a = self.c.get("_arenas", {}).get(name)
        if a is None:
            return name
        isz = 2 if ap.dtype == BF16 else 4
        bcol = (ap.offset * isz) % a["rowbytes"]
        for rn, (p0, np_, b0, nb) in a["regions"].items():
            if b0 <= bcol < b0 + nb:
                return rn
        return name

    def carve(self, aname, name, p0, np_, b0, nbytes, dtype, free=()):
        a = self.c["_arenas"][aname]
        regions = a["regions"]
        for f in free:
            regions.pop(f, None)
        arena = a["t"]
        assert b0 % 32 == 0 and b0 + nbytes <= arena.shape[1] * 2, (name, b0, nbytes)
        for rn, (q0, nq, c0, nc) in list(regions.items()):
            if rn == name:
                continue
            if p0 < q0 + nq and q0 < p0 + np_ and b0 < c0 + nc and c0 < b0 + nbytes:
                raise AssertionError(f"arena overlap: {name} vs live {rn}")
        olds = [rn for (rn, q0, nq, c0, nc) in a["hist"]
                if rn != name and p0 < q0 + nq and q0 < p0 + np_
                and b0 < c0 + nc and c0 < b0 + nbytes]
        if olds:
            self.p.inherit(name, olds)
        regions[name] = (p0, np_, b0, nbytes)
        a["hist"].append((name, p0, np_, b0, nbytes))
        if dtype == BF16:
            ap = arena[p0:p0 + np_, b0 // 2: b0 // 2 + nbytes // 2]
        elif dtype == F32:
            ap = arena[p0:p0 + np_, b0 // 2: b0 // 2 + nbytes // 2].bitcast(F32)
        else:
            raise AssertionError(dtype)
        return ap

    def tiles(self, setname):
        key = f"_tiles_{setname}"
        if key not in self.c:
            self.c[key] = [
                self.carve(f"set{setname}", f"{setname}{i}", 0, 128,
                           i * 4096, 4096, BF16) for i in range(8)]
        return self.c[key]

    def retile(self, setname):
        self.c.pop(f"_tiles_{setname}", None)
        a = self.c["_arenas"][f"set{setname}"]
        big = [rn for rn in a["regions"] if not rn.startswith(setname)]
        for rn in big:
            a["regions"].pop(rn)
        return self.tiles(setname)

    def debug(self, name, ap, reads=None):
        dbg = self.c.get("_debug")
        if not dbg:
            return
        if dbg is not True and name not in dbg:
            if dbg is not True and f"nop:{name}" in dbg:
                self.p.v(lambda e: e.nop(), reads=list(reads or []),
                         writes=list(reads or []), tag=f"probe_{name}")
                return
            if dbg is not True and f"gp:{name}" in dbg:
                shp = list(ap.shape)
                d = self.nc.dram_tensor(f"dbg_{name}", shp, ap.dtype,
                                        kind="ExternalOutput").ap()
                self.p.dma("gpsimd", d, ap, extra_reads=(reads or []),
                           tag=f"dbg_{name}")
                return
            return
        shp = list(ap.shape)
        d = self.nc.dram_tensor(f"dbg_{name}", shp, ap.dtype,
                                kind="ExternalOutput").ap()
        self.p.dma("sync", d, ap, extra_reads=(reads or []), tag=f"dbg_{name}")

    def banks(self):
        if "pbank0" not in self.c:
            for i in range(7):
                self.c[f"pbank{i}"] = self.ps(f"pbank{i}", [128, 512], F32)
            self.c["pbankT"] = self.ps("pbankT", [128, 1024], BF16)
        return [self.c[f"pbank{i}"] for i in range(7)] + [self.c["pbankT"]]




CONSTVALS = [1e-5, 1.0, -60.0, -0.5, 0.125, -1.0, 1e12, 0.0, 8.0, 2.0]


def emit_setup_consts(ctx):
    p, c = ctx.p, ctx.c
    nc = ctx.nc

    def ld(queue, key, name, shape, dtype):
        dst = ctx.sb(key, shape, dtype)
        c[key] = dst
        src = nc.dram_tensor(name, shape, dtype, kind="ExternalInput").ap()
        p.dma(queue, dst[:], src, tag=f"ld_{name}")
        return dst

    cv = ld("sync", "constvals", "hc_constvals", [128, 10], F32)
    ld("sync", "ones_col_bf", "hc_ones_col", [128, 1], BF16)
    ld("sync", "ones_row_f32", "hc_ones_row", [65, 128], F32)
    ld("sync", "ones_row_bf", "hc_ones_rowb", [65, 128], BF16)
    if ctx.c.get("_need_idents", True):
        ld("sync", "ident_bf", "hc_ident", [128, 128], BF16)
        ld("sync", "ident_f32", "hc_identf", [128, 128], F32)
    for i, val in enumerate(CONSTVALS):
        nc.const_aps.register(F32, float(val), cv[:, i:i + 1])


def host_consts():
    bf = ml_dtypes.bfloat16
    return {
        "hc_constvals": np.tile(np.array(CONSTVALS, np.float32), (128, 1)),
        "hc_ones_col": np.ones((128, 1), bf),
        "hc_ones_row": np.ones((65, 128), np.float32),
        "hc_ones_rowb": np.ones((65, 128), bf),
        "hc_ident": np.eye(128, dtype=np.float32).astype(bf),
        "hc_identf": np.eye(128, dtype=np.float32),
    }


def emit_ln_stats(ctx, src_tiles, xbf, sq_tiles, tag, r32row=None,
                  shift_half=False):
    p, c = ctx.p, ctx.c
    banks = ctx.banks()
    rows = ctx.pool("ln_rows", [1, 2560], F32)
    RK = "ln_rows"
    r_bf = rows[0:1, 0:1024].bitcast(BF16)
    scr = rows[0:1, 1024:1536]
    negm_bf = rows[0:1, 1536:2560].bitcast(BF16)
    ones_col = c["ones_col_bf"]
    for i in range(8):
        sh = shift_half and i >= 4
        def cp(e, a=src_tiles[i], b=xbf[i], sh=sh):
            if sh:
                return e.copy(b[:, 1:T], a[:, 0:T - 1])
            return e.copy(b[:], a[:])
        def sq(e, a=src_tiles[i], b=sq_tiles[i], sh=sh):
            if sh:
                return e.activation(b[:, 1:T], a[:, 0:T - 1], AF.Square)
            return e.activation(b[:], a[:], AF.Square)
        p.s(cp, reads=[src_tiles[i]], writes=[xbf[i]], tag=f"{tag}cast")
        p.s(sq, reads=[src_tiles[i]], writes=[sq_tiles[i]], tag=f"{tag}sq")
        if sh:
            def z1(e, b=xbf[i]):
                return e.memset(b[:, 0:1], 0.0)
            def z2(e, b=sq_tiles[i]):
                return e.memset(b[:, 0:1], 0.0)
            p.v(z1, reads=[], writes=[xbf[i]], tag=f"{tag}z1")
            p.v(z2, reads=[], writes=[sq_tiles[i]], tag=f"{tag}z2")
    ps_sum = banks[0][0:1, :]
    ps_sq = banks[1][0:1, :]
    for cch in range(4):
        cs = slice(cch * 512, cch * 512 + 512)
        for i in range(8):
            def mm(e, b=xbf[i], cs=cs, st=(i == 0), sp=(i == 7)):
                return e.matmul(ps_sum, ones_col[:], b[:, cs], start=st, stop=sp)
            p.t(mm, reads=[xbf[i], ones_col], writes=[banks[0]], tag=f"{tag}mS")
        for i in range(8):
            def mm(e, b=sq_tiles[i], cs=cs, st=(i == 0), sp=(i == 7)):
                return e.matmul(ps_sq, ones_col[:], b[:, cs], start=st, stop=sp)
            p.t(mm, reads=[sq_tiles[i], ones_col], writes=[banks[1]], tag=f"{tag}mQ")
        p.v(lambda e: e.tensor_scalar(ps_sum, ps_sum, 1.0 / D, None, ALU.mult),
            reads=[banks[0]], writes=[banks[0]], tag=f"{tag}mean")
        p.v(lambda e: e.tensor_scalar(ps_sq, ps_sq, 1.0 / D, None, ALU.mult),
            reads=[banks[1]], writes=[banks[1]], tag=f"{tag}ex2")
        p.s(lambda e: e.activation(scr, ps_sum, AF.Square),
            reads=[banks[0]], writes=[RK], tag=f"{tag}m2")
        p.v(lambda e: e.tensor_tensor(ps_sq, ps_sq, scr, ALU.subtract),
            reads=[banks[1], RK], writes=[banks[1]], tag=f"{tag}var")
        p.s(lambda e: e.activation(ps_sq, ps_sq, AF.Ln, bias=1e-5),
            reads=[banks[1]], writes=[banks[1]], tag=f"{tag}lnf")
        p.s(lambda e: e.activation(ps_sq, ps_sq, AF.Exp, scale=-0.5),
            reads=[banks[1]], writes=[banks[1]], tag=f"{tag}expf")
        def evr(e, cs=cs):
            return e.copy(r_bf[0:1, cs], ps_sq)
        p.s(evr, reads=[banks[1]], writes=[RK], tag=f"{tag}evr")
        def evm(e, cs=cs):
            return e.tensor_scalar(negm_bf[0:1, cs], ps_sum, -1.0, None,
                                   ALU.mult)
        p.v(evm, reads=[banks[0]], writes=[RK], tag=f"{tag}evm")
        if r32row is not None:
            def evr32(e, cs=cs):
                return e.tensor_copy(r32row[0:1, cs], ps_sq)
            p.v(evr32, reads=[banks[1]], writes=[r32row], tag=f"{tag}evr32")
    ident_bf = c["ident_bf"]
    r_tm = ctx.pool("lnr_tm", [128, NT], F32)
    pst = banks[7]
    for tt in range(NT):
        def tp(e, tt=tt):
            return e.transpose(pst[:, 0:1], r_bf[0:1, tt * 128:tt * 128 + 128],
                               ident_bf[0:1, 0:1])
        p.t(tp, reads=[RK, ident_bf], writes=[banks[7]], tag=f"{tag}tp")
        def ev(e, tt=tt):
            return e.tensor_copy(r_tm[:, tt:tt + 1], pst[:, 0:1])
        p.v(ev, reads=[banks[7]], writes=[r_tm], tag=f"{tag}evtm")
    return dict(RK=RK, negm_bf=negm_bf, r_bf=r_bf, r_tm=r_tm)


def emit_colsum(ctx, wslot, tag, n_out=D, wcol=None, wcol_key=None,
                region=(1024, 1536)):
    p, c = ctx.p, ctx.c
    banks = ctx.banks()
    ones_col = wcol if wcol is not None else c["ones_col_bf"]
    wk = [wcol_key] if wcol_key else []
    nch = (n_out + 511) // 512
    assert n_out <= (region[1] - region[0]) * 2
    rows_t = ctx.pool("ln_rows", [1, 2560], F32)
    dst = rows_t[0:1, region[0]:region[1]].bitcast(BF16)[0:1, 0:n_out]
    KEY = "ln_rows"
    ps = banks[2][0:1, :]
    for cch in range(nch):
        w = min(512, n_out - cch * 512)
        for kt in range(8):
            def mm(e, kt=kt, cch=cch, w=w, st=(kt == 0), sp=(kt == 7)):
                return e.matmul(ps[:, 0:w], ones_col[:],
                                wslot[:, kt * n_out + cch * 512:
                                      kt * n_out + cch * 512 + w],
                                start=st, stop=sp)
            p.t(mm, reads=["Wslot", ones_col] + wk, writes=[banks[2]],
                tag=f"{tag}cs")
        def ev(e, cch=cch, w=w):
            return e.copy(dst[0:1, cch * 512:cch * 512 + w], ps[:, 0:w])
        p.s(ev, reads=[banks[2]], writes=[KEY], tag=f"{tag}csev")
    return dst, KEY




def emit_l0_attn(ctx, x_tiles):
    p, c = ctx.p, ctx.c
    nc = ctx.nc
    NHEADS = 16
    banks = ctx.banks()

    wslot = ctx.pool("Wslot", [128, 8 * D], BF16)
    diag4 = wslot[:, 0:512]
    DGK = "Wsub_diag4"
    WSUBS = ["Wsub_pt0", "Wsub_pt1", "Wsub_rbc", "Wsub_rrow", "Wsub_rrowbf",
             DGK]
    xbf = ctx.tiles("B")
    sqt = ctx.tiles("C")
    ln = emit_ln_stats(ctx, x_tiles, xbf, sqt, "l0aln")
    RK = ln["RK"]
    negm_bf, r_tm, r_bf = ln["negm_bf"], ln["r_tm"], ln["r_bf"]

    ctx.debug("rbf", ln["r_bf"], reads=[RK])
    ctx.debug("negmbf", ln["negm_bf"], reads=[RK])
    ctx.debug("rtm", r_tm[:], reads=[r_tm])
    ctx.debug("xbf0", xbf[0][:], reads=[xbf[0]])
    wq_d = nc.dram_tensor("L0_WqT", [D, D], F32, kind="ExternalInput").ap()
    wk_d = nc.dram_tensor("L0_WkT", [D, D], F32, kind="ExternalInput").ap()
    wv_d = nc.dram_tensor("L0_WvT", [D, D], F32, kind="ExternalInput").ap()
    wo_d = nc.dram_tensor("L0_WoT", [D, D], F32, kind="ExternalInput").ap()

    def wload(src, subs=WSUBS):
        for k in range(8):
            def dm(e, k=k, src=src):
                return e.dma_start(out=wslot[:, k * D:(k + 1) * D],
                                   in_=src[k * 128:(k + 1) * 128, :])
            p.op("gpsimd", dm, reads=[], writes=[wslot] + list(subs),
                 dma=True, tag="ldW")

    q_fm = ctx.tiles("A")
    k_fm = sqt
    rbc_sb = ctx.pool("rbc_sb", [128, 512], BF16)
    onesrb = c["ones_row_bf"]
    for dst, wd in ((q_fm, wq_d), (k_fm, wk_d)):
        wload(wd)
        colsum, CSK = emit_colsum(ctx, wslot, "l0a")
        for cch in range(4):
            cs = slice(cch * 512, cch * 512 + 512)
            def bc(e, cs=cs):
                return e.matmul(banks[2][:], onesrb[0:1, :], r_bf[0:1, cs],
                                start=True, stop=True)
            p.t(bc, reads=[RK, onesrb], writes=[banks[2]], tag="l0a_rbb")
            def bcc(e):
                return e.tensor_copy(rbc_sb[:], banks[2][:])
            p.v(bcc, reads=[banks[2]], writes=[rbc_sb], tag="l0a_rbcc")
            for mt in range(8):
                ps = banks[4 + mt % 2]
                for kt in range(8):
                    def mm(e, ps=ps, kt=kt, mt=mt, cs=cs,
                           st=(kt == 0)):
                        return e.matmul(
                            ps[:],
                            wslot[:, kt * D + mt * 128: kt * D + mt * 128 + 128],
                            xbf[kt][:, cs], start=st, stop=False)
                    p.t(mm, reads=["Wslot", xbf[kt]], writes=[ps], tag="l0a_qk")
                def mm1(e, ps=ps, mt=mt, cs=cs):
                    return e.matmul(ps[:],
                                    colsum[0:1, mt * 128:mt * 128 + 128],
                                    negm_bf[0:1, cs], start=False, stop=True)
                p.t(mm1, reads=[CSK, "Wsub_negmbf"], writes=[ps], tag="l0a_qk1")
                def ev(e, ps=ps, d=dst[mt], cs=cs):
                    return e.tensor_tensor(d[:, cs], ps[:], rbc_sb[:], ALU.mult)
                p.v(ev, reads=[ps, rbc_sb], writes=[dst[mt]], tag="l0a_qkev")

    ctx.debug("q0", q_fm[0][:], reads=[q_fm[0]])
    ctx.debug("k0", k_fm[0][:], reads=[k_fm[0]])
    spill = c["spill_dram"]
    xt = c["x_tiles"]
    for i in range(3):
        p.dma("sync", spill[i], xt[i][:], tag="spill_x")
    v_ext = [ctx.carve("arena", f"l0a_vx{tt}", 0, 128, tt * 2080, 2080,
                       BF16, free=("x0", "x1", "x2") if tt == 0 else ())
             for tt in range(NT)]
    VXK = [f"l0a_vx{tt}" for tt in range(NT)]
    for tt in range(NT):
        def ms(e, vx=v_ext[tt]):
            v3 = vx.rearrange("p (h o) -> p h o", h=NHEADS)
            return e.memset(v3[:, :, 64:65], 1.0)
        p.v(ms, reads=[], writes=[VXK[tt]], tag="l0a_vones")
    wload(wv_d)
    colsum, CSK = emit_colsum(ctx, wslot, "l0av")
    for tt in range(NT):
        for half in range(2):
            ps = banks[4 + half]
            for kt in range(8):
                def mm(e, ps=ps, tt=tt, kt=kt, half=half, st=(kt == 0)):
                    return e.matmul(
                        ps[:],
                        xbf[kt][:, tt * 128: tt * 128 + 128],
                        wslot[:, kt * D + half * 512: kt * D + half * 512 + 512],
                        start=st, stop=False)
                p.t(mm, reads=["Wslot", xbf[kt]], writes=[ps], tag="l0a_v")
            def mm1(e, ps=ps, tt=tt, half=half):
                return e.matmul(ps[:],
                                negm_bf[0:1, tt * 128:tt * 128 + 128],
                                colsum[0:1, half * 512:half * 512 + 512],
                                start=False, stop=True)
            p.t(mm1, reads=[CSK, "Wsub_negmbf"], writes=[ps], tag="l0a_v1")
            def ev(e, ps=ps, vx=v_ext[tt], half=half, tt=tt):
                dst = vx[:, half * 8 * 65: half * 8 * 65 + 8 * 65]
                dst3 = dst.rearrange("p (h o) -> p h o", h=8)[:, :, 0:64]
                src3 = ps[:].rearrange("p (h o) -> p h o", h=8)
                return e.tensor_scalar(dst3, src3, r_tm[:, tt:tt + 1],
                                       None, ALU.mult)
            p.v(ev, reads=[ps, r_tm], writes=[VXK[tt]], tag="l0a_vev")

    ctx.debug("vx0", v_ext[0], reads=[VXK[0]])
    pts = [wslot[:, 0:512], wslot[:, 512:1024]]
    PTK = ["Wsub_pt0", "Wsub_pt1"]
    o_fm = xbf
    rbc_bf = wslot[:, 1024:1536]
    rrow = wslot[0:1, 1536:2560].bitcast(F32)
    rrow_bf = wslot[0:1, 2560:3072]

    for h in range(NHEADS):
        hp, ho = h // 2, (h % 2) * 64
        for qc in range(4):
            po = banks[6]
            for kt in range(NT):
                ps = banks[4 + kt % 2]
                pt = pts[kt % 2]
                ptk = PTK[kt % 2]
                def mmS(e, ps=ps, hp=hp, ho=ho, kt=kt, qc=qc):
                    return e.matmul(
                        ps[:],
                        k_fm[hp][ho:ho + 64, kt * 128: kt * 128 + 128],
                        q_fm[hp][ho:ho + 64, qc * 512: qc * 512 + 512],
                        start=True, stop=True)
                p.t(mmS, reads=[k_fm[hp], q_fm[hp]], writes=[ps], tag="l0a_S")
                def ex(e, ps=ps, pt=pt):
                    return e.activation(pt, ps[:], AF.Exp, scale=0.125)
                p.s(ex, reads=[ps], writes=[ptk], tag="l0a_exp")
                def mmO(e, po=po, pt=pt, tt=kt, h=h,
                        st=(kt == 0), sp=(kt == NT - 1)):
                    return e.matmul(po[0:65, :], v_ext[tt][:, h * 65: h * 65 + 65],
                                    pt, start=st, stop=sp)
                p.t(mmO, reads=[ptk, VXK[kt]], writes=[po], tag="l0a_O")
            def rc(e, po=po):
                return e.reciprocal(rrow, po[64:65, :])
            p.v(rc, reads=[po], writes=["Wsub_rrow"], tag="l0a_rec")
            def rcb(e):
                return e.copy(rrow_bf, rrow)
            p.s(rcb, reads=["Wsub_rrow"], writes=["Wsub_rrowbf"], tag="l0a_recb")
            def bc(e):
                return e.matmul(banks[5][:, :], onesrb[0:1, :], rrow_bf,
                                start=True, stop=True)
            p.t(bc, reads=["Wsub_rrowbf", onesrb], writes=[banks[5]], tag="l0a_rbc")
            def cpb(e):
                return e.tensor_copy(rbc_bf, banks[5][:, :])
            p.v(cpb, reads=[banks[5]], writes=["Wsub_rbc"], tag="l0a_rbccp")
            def dv(e, po=po, dst=o_fm[hp], ho=ho, qc=qc):
                return e.tensor_tensor(dst[ho:ho + 64, qc * 512: qc * 512 + 512],
                                       po[0:64, :], rbc_bf[ho:ho + 64, :],
                                       ALU.mult)
            p.v(dv, reads=[po, "Wsub_rbc"], writes=[o_fm[hp]], tag="l0a_div")

    ctx.debug("o0", o_fm[0][:], reads=[o_fm[0]])
    ctx.debug("o7", o_fm[7][:], reads=[o_fm[7]])
    wload(wo_d)
    for i in range(3):
        ctx.carve("arena", f"x{i}", 0, 128, i * 8192, 8192, F32,
                  free=tuple(VXK) if i == 0 else ())
        p.dma("sync", xt[i][:], spill[i], tag="fill_x")
    for mt in range(8):
        for cch in range(4):
            ps = banks[4 + (mt * 4 + cch) % 2]
            for kt in range(8):
                def mm(e, ps=ps, kt=kt, mt=mt, cch=cch,
                       st=(kt == 0), sp=(kt == 7)):
                    return e.matmul(
                        ps[:],
                        wslot[:, kt * D + mt * 128: kt * D + mt * 128 + 128],
                        o_fm[kt][:, cch * 512:cch * 512 + 512],
                        start=st, stop=sp)
                p.t(mm, reads=["Wslot", o_fm[kt]], writes=[ps], tag="l0a_wo")
            def acc(e, ps=ps, xt=x_tiles[mt], cch=cch):
                return e.tensor_tensor(xt[:, cch * 512:cch * 512 + 512],
                                       xt[:, cch * 512:cch * 512 + 512],
                                       ps[:], ALU.add)
            p.v(acc, reads=[ps, x_tiles[mt]], writes=[x_tiles[mt]], tag="l0a_res")




def emit_l0_ffn(ctx, x_tiles):
    _stop = ctx.c.get("_l0f_stop", 99)
    """x += MoE(LN(x)).  Per (group g of 16 tokens, expert-slot es of 64):
    the argmax token t* of logits gets FFN_es(xn[t*]) added."""
    p, c = ctx.p, ctx.c
    nc = ctx.nc
    banks = ctx.banks()
    NES, NBLK, ESB = 64, 4, 16

    xbf = ctx.retile("B")
    sqt = ctx.retile("C")
    r32 = ctx.carve("arena", "r32", 0, 128, 24576, 8192, F32,
                    free=[f"l0a_vx{tt}" for tt in range(3, NT)])[0:1, :]
    ln = emit_ln_stats(ctx, x_tiles, xbf, sqt, "l0f", r32row=r32)
    RK, negm_bf, r_bf, r_tm = ln["RK"], ln["negm_bf"], ln["r_bf"], ln["r_tm"]

    xlo = sqt
    for i in range(8):
        def sub(e, a=x_tiles[i], b=xbf[i], o=xlo[i]):
            return e.tensor_tensor(o[:], a[:], b[:], ALU.subtract)
        p.v(sub, reads=[x_tiles[i], xbf[i]], writes=[xlo[i]], tag="l0fxlo")

    if _stop <= 1:
        return
    spill = c["spill_dram"]
    for i in range(3):
        p.dma("sync", spill[i], x_tiles[i][:], tag="l0f_spill")
    lg = ctx.carve("arena", "lg", 0, 64, 0, 8192, F32, free=("x0", "x1", "x2"))
    cstage = ctx.carve("arena", "cstage", 0, 128, 8192, 2048, F32)
    chi = ctx.carve("arena", "chi", 0, 128, 10240, 1024, BF16)
    clo = ctx.carve("arena", "clo", 0, 128, 11264, 1024, BF16)
    pem = ctx.carve("arena", "pem", 0, 64, 12288, 4096, BF16)
    rbc32 = ctx.carve("arena", "rbc32", 0, 128, 16384, 2048, F32)
    tie = ctx.carve("arena", "tierow", 0, 1, 18432, 4096, BF16)

    ctrl_d = nc.dram_tensor("L0F_ctrl", [D, NES], F32, kind="ExternalInput").ap()
    for kt in range(8):
        def dm(e, kt=kt):
            return e.dma_start(out=cstage[:, kt * 64:(kt + 1) * 64],
                               in_=ctrl_d[kt * 128:(kt + 1) * 128, :])
        p.op("sync", dm, reads=[], writes=["cstage"], dma=True, tag="ldctrl")
    p.v(lambda e: e.tensor_copy(chi, cstage), reads=["cstage"], writes=["chi"])
    p.v(lambda e: e.tensor_tensor(clo, cstage, chi, ALU.subtract),
        reads=["cstage", "chi"], writes=["clo"])
    rows_t = ctx.pool("ln_rows", [1, 2560], F32)
    ccsum = rows_t[0:1, 1024:1536].bitcast(BF16)[0:1, 0:64]
    ps2 = banks[2][0:1, :]
    for kt in range(8):
        def mm(e, kt=kt, st=(kt == 0), sp=(kt == 7)):
            return e.matmul(ps2[:, 0:64], c["ones_col_bf"][:],
                            chi[:, kt * 64:(kt + 1) * 64], start=st, stop=sp)
        p.t(mm, reads=["chi", c["ones_col_bf"]], writes=[banks[2]], tag="l0fcs")
    p.s(lambda e: e.copy(ccsum, ps2[:, 0:64]), reads=[banks[2]],
        writes=["ln_rows"], tag="l0fcsev")
    tie_d = nc.dram_tensor("hc_tie", [1, T], BF16, kind="ExternalInput").ap()
    p.dma("sync", tie, tie_d, tag="ldtie")

    onesr = c["ones_row_f32"]
    onesrb = c["ones_row_bf"]
    for cch in range(4):
        cs = slice(cch * 512, cch * 512 + 512)
        ps = banks[0][0:64, :]
        for kt in range(8):
            def mm1(e, kt=kt, cs=cs, st=(kt == 0)):
                return e.matmul(ps, chi[:, kt * 64:(kt + 1) * 64],
                                xbf[kt][:, cs], start=st, stop=False)
            p.t(mm1, reads=["chi", xbf[kt]], writes=[banks[0]], tag="l0flg1")
        for kt in range(8):
            def mm2(e, kt=kt, cs=cs):
                return e.matmul(ps, chi[:, kt * 64:(kt + 1) * 64],
                                xlo[kt][:, cs], start=False, stop=False)
            p.t(mm2, reads=["chi", xlo[kt]], writes=[banks[0]], tag="l0flg2")
        for kt in range(8):
            def mm3(e, kt=kt, cs=cs):
                return e.matmul(ps, clo[:, kt * 64:(kt + 1) * 64],
                                xbf[kt][:, cs], start=False, stop=False)
            p.t(mm3, reads=["clo", xbf[kt]], writes=[banks[0]], tag="l0flg3")
        def mm4(e, cs=cs):
            return e.matmul(ps, ccsum, negm_bf[0:1, cs], start=False,
                            stop=True)
        p.t(mm4, reads=["ln_rows", RK], writes=[banks[0]], tag="l0flg4")
        def bc(e, cs=cs):
            return e.matmul(banks[2][:], onesr[0:1, :], r32[0:1, cs],
                            start=True, stop=True)
        p.t(bc, reads=["r32", onesr], writes=[banks[2]], tag="l0flgbc")
        p.v(lambda e: e.tensor_copy(rbc32, banks[2][:]),
            reads=[banks[2]], writes=["rbc32"], tag="l0flgbcc")
        def evl(e, cs=cs):
            return e.tensor_tensor(lg[0:64, cs], ps, rbc32[0:64, :], ALU.mult)
        p.v(evl, reads=[banks[0], "rbc32"], writes=["lg"], tag="l0flgev")
        def bct(e, cs=cs):
            return e.matmul(banks[2][0:64, :], onesrb[0:1, 0:64], tie[0:1, cs],
                            start=True, stop=True)
        p.t(bct, reads=["tierow", onesrb], writes=[banks[2]], tag="l0ftiebc")
        def addt(e, cs=cs):
            return e.tensor_tensor(lg[0:64, cs], lg[0:64, cs],
                                   banks[2][0:64, :], ALU.add)
        p.v(addt, reads=["lg", banks[2]], writes=["lg"], tag="l0ftieadd")

    if _stop <= 2:
        for i in range(3):
            ctx.carve("arena", f"x{i}", 0, 128, i * 8192, 8192, F32,
                      free=("lg", "cstage", "chi", "clo", "pem", "rbc32",
                            "tierow", "r32") if i == 0 else ())
            p.dma("sync", x_tiles[i][:], spill[i], tag="l0f_fill")
        return
    ctx.debug("lg", lg, reads=["lg"])
    lgt8 = ctx.carve("arena", "lgt8", 0, 64, 8192, 4096,
                     F32, free=("cstage", "chi", "clo"))
    lgt421 = ctx.carve("arena", "lgt421", 0, 64, 18432, 4096, F32,
                       free=("tierow",))
    lg3 = lg.rearrange("p (g t) -> p g t", g=128)
    m8 = lgt8.rearrange("p (g t) -> p g t", g=128)
    p.v(lambda e: e.tensor_tensor(m8, lg3[:, :, 0:8], lg3[:, :, 8:16],
                                  ALU.max), reads=["lg"], writes=["lgt8"],
        tag="l0fmx1")
    m4 = lgt421[:, 0:512].rearrange("p (g t) -> p g t", g=128)
    p.v(lambda e: e.tensor_tensor(m4, m8[:, :, 0:4], m8[:, :, 4:8], ALU.max),
        reads=["lgt8"], writes=["lgt421"], tag="l0fmx2")
    m2 = lgt421[:, 512:768].rearrange("p (g t) -> p g t", g=128)
    p.v(lambda e: e.tensor_tensor(m2, m4[:, :, 0:2], m4[:, :, 2:4], ALU.max),
        reads=["lgt421"], writes=["lgt421"], tag="l0fmx3")
    m1 = lgt421[:, 768:896].rearrange("p (g t) -> p g t", g=128)
    p.v(lambda e: e.tensor_tensor(m1, m2[:, :, 0:1], m2[:, :, 1:2], ALU.max),
        reads=["lgt421"], writes=["lgt421"], tag="l0fmx4")
    gmax_x = m1.broadcast_to([64, 128, 16])
    p.v(lambda e: e.tensor_tensor(pem.rearrange("p (g t) -> p g t", g=128),
                                  lg3, gmax_x, ALU.is_equal),
        reads=["lg", "lgt421"], writes=["pem"], tag="l0fcmp")

    if _stop <= 3:
        for i in range(3):
            ctx.carve("arena", f"x{i}", 0, 128, i * 8192, 8192, F32,
                      free=("lg", "cstage", "chi", "clo", "pem", "rbc32",
                            "tierow", "r32", "lgt8", "lgt421") if i == 0
                      else ())
            p.dma("sync", x_tiles[i][:], spill[i], tag="l0f_fill")
        return
    ctx.debug("pem", pem, reads=["pem"])
    pstb = banks[7]
    perm_tm = ctx.pool("perm_tm", [128, NT * 64], BF16)
    for tt in range(NT):
        def tp(e, tt=tt):
            return e.transpose(pstb[:, 0:64], pem[:, tt * 128:tt * 128 + 128],
                               c["ident_bf"][0:64, 0:64])
        p.t(tp, reads=["pem", c["ident_bf"]], writes=[banks[7]], tag="l0fptp")
        def ev(e, tt=tt):
            return e.tensor_copy(perm_tm[:, tt * 64:tt * 64 + 64],
                                 pstb[:, 0:64])
        p.v(ev, reads=[banks[7]], writes=[perm_tm], tag="l0fptpev")
    p.v(lambda e: e.tensor_tensor(negm_bf, negm_bf, r_bf, ALU.mult),
        reads=[RK], writes=[RK], tag="l0fnmr")
    nmr_tm = ctx.pool("nmr_tm", [128, NT], BF16)
    for tt in range(NT):
        def tp(e, tt=tt):
            return e.transpose(pstb[:, 64:65],
                               negm_bf[0:1, tt * 128:tt * 128 + 128],
                               c["ident_bf"][0:1, 0:1])
        p.t(tp, reads=[RK, c["ident_bf"]], writes=[banks[7]], tag="l0ftpn")
        def ev(e, tt=tt):
            return e.tensor_copy(nmr_tm[:, tt:tt + 1], pstb[:, 64:65])
        p.v(ev, reads=[banks[7]], writes=[nmr_tm], tag="l0fevn")

    xtm = [ctx.carve("setA", f"xtm{t}", 0, 128, t * 2048, 2048, BF16,
                     free=[f"A{i}" for i in range(8)] if t == 0 else ())
           for t in range(NT)]
    for dt in range(8):
        for tt in range(NT):
            def tp(e, dt=dt, tt=tt):
                return e.transpose(pstb[:, 0:128],
                                   xbf[dt][:, tt * 128:tt * 128 + 128],
                                   c["ident_bf"][:])
            p.t(tp, reads=[xbf[dt], c["ident_bf"]], writes=[banks[7]],
                tag="l0ftp")
            def ev(e, dt=dt, tt=tt):
                return e.tensor_copy(xtm[tt][:, dt * 128:dt * 128 + 128],
                                     pstb[:, 0:128])
            p.v(ev, reads=[banks[7]], writes=[xtm[tt]], tag="l0ftpev")

    if _stop <= 4:
        for i in range(3):
            ctx.carve("arena", f"x{i}", 0, 128, i * 8192, 8192, F32,
                      free=("lg", "lgt8", "lgt421", "pem", "rbc32") if i == 0
                      else ())
            p.dma("sync", x_tiles[i][:], spill[i], tag="l0f_fill")
        return
    for i in range(3):
        ctx.carve("arena", f"x{i}", 0, 128, i * 8192, 8192, F32,
                  free=("lg", "lgt8", "lgt421", "pem", "rbc32") if i == 0
                  else ())
        p.dma("sync", x_tiles[i][:], spill[i], tag="l0f_fill")
    Rr = ctx.carve("arena", "Rr", 0, 128, 24576, 4096, BF16, free=("r32",))
    R01T = ctx.carve("arena", "R01T", 0, 128, 28672, 4096, BF16)

    wslot = ctx.pool("Wslot", [128, 8 * D], BF16)
    diag4 = wslot[:, 0:512]
    DGK = "Wsub_diag4"
    WSUBS = ["Wsub_pt0", "Wsub_pt1", "Wsub_rbc", "Wsub_rrow", "Wsub_rrowbf",
             DGK]
    f1_d = nc.dram_tensor("L0F_f1", [NBLK, 8, 128, ESB * 64], F32,
                          kind="ExternalInput").ap()
    f2_d = nc.dram_tensor("L0F_f2", [NES, 64, D], F32,
                          kind="ExternalInput").ap()
    bias_d = nc.dram_tensor("L0F_biasT", [64, NES], F32,
                            kind="ExternalInput").ap()
    biasT = ctx.pool("l0f_biasT", [64, NES], F32)
    p.dma("sync", biasT[:], bias_d, tag="ldbias")
    glmask_d = nc.dram_tensor("hc_glmask", [128, 128], BF16,
                              kind="ExternalInput").ap()
    glmask = ctx.pool("glmask", [128, 128], BF16)
    p.dma("sync", glmask[:], glmask_d, tag="ldglm")
    r01tmp = ctx.pool("l0f_r01tmp", [128, 128], BF16)
    nmrsel = rows_t[0:1, 0:1024].bitcast(BF16)
    inner = [ctx.pool(f"l0f_inner{i}", [64, 128], BF16) for i in range(ESB)]
    restripe_d = nc.dram_tensor("restripe", [ESB * 8 * NT * D], BF16,
                                kind="Internal").ap()
    rs_kged = restripe_d.rearrange("(e g k d) -> k g e d", e=ESB, g=8, k=NT)
    rs_egkd = restripe_d.rearrange("(e g k d) -> (e g) k d", e=ESB, g=8, k=NT)

    for blk in range(NBLK if _stop > 5 else 1):
        for kt in range(8):
            def dm(e, kt=kt, blk=blk):
                return e.dma_start(out=wslot[:, kt * 1024:(kt + 1) * 1024],
                                   in_=f1_d[blk, kt])
            p.op("gpsimd", dm, reads=[], writes=[wslot] + WSUBS, dma=True,
                 tag="ldf1")
        csum1, CSK1 = emit_colsum(ctx, wslot, f"l0fb{blk}", n_out=1024)
        sel = ctx.retile("B")
        for kt in range(NT):
            pslice = perm_tm[:, kt * 64 + blk * ESB: kt * 64 + blk * ESB + ESB]
            pexp = pslice.unsqueeze(2).broadcast_to([128, ESB, 8])
            def bR(e, pexp=pexp, kt=kt):
                return e.scalar_tensor_tensor(
                    Rr[:, kt * 128:kt * 128 + 128].rearrange(
                        "p (a b) -> p a b", a=ESB),
                    pexp, r_tm[:, kt:kt + 1],
                    glmask[:].rearrange("p (a b) -> p a b", a=ESB),
                    ALU.mult, ALU.mult)
            p.v(bR, reads=[perm_tm, r_tm, glmask], writes=["Rr"], tag="l0fbR")
            def b01(e, pexp=pexp, kt=kt):
                return e.tensor_tensor(
                    r01tmp[:].rearrange("p (a b) -> p a b", a=ESB),
                    pexp, glmask[:].rearrange("p (a b) -> p a b", a=ESB),
                    ALU.mult)
            p.v(b01, reads=[perm_tm, glmask], writes=[r01tmp], tag="l0fb01")
            def tpR(e, kt=kt):
                return e.transpose(pstb[:, 0:128], r01tmp[:], c["ident_bf"][:])
            p.t(tpR, reads=[r01tmp, c["ident_bf"]], writes=[banks[7]],
                tag="l0ftpR")
            def evR(e, kt=kt):
                return e.tensor_copy(R01T[:, kt * 128:kt * 128 + 128],
                                     pstb[:, 0:128])
            p.v(evR, reads=[banks[7]], writes=["R01T"], tag="l0fevR")
            def mmn(e, kt=kt):
                return e.matmul(banks[2][0:1, 0:128], nmr_tm[:, kt:kt + 1],
                                r01tmp[:], start=True, stop=True)
            p.t(mmn, reads=[nmr_tm, r01tmp], writes=[banks[2]], tag="l0fnm")
            def evn(e, kt=kt):
                return e.copy(nmrsel[0:1, kt * 128:kt * 128 + 128],
                              banks[2][0:1, 0:128])
            p.s(evn, reads=[banks[2]], writes=["ln_rows"], tag="l0fnmev")
            for dt in range(8):
                psg = banks[4 + dt % 2]
                def mmg(e, dt=dt, kt=kt, psg=psg):
                    return e.matmul(psg[:, 0:128],
                                    xtm[kt][:, dt * 128:dt * 128 + 128],
                                    Rr[:, kt * 128:kt * 128 + 128],
                                    start=True, stop=True)
                p.t(mmg, reads=[f"xtm{kt}", "Rr"], writes=[psg], tag="l0fg")
                def evg(e, dt=dt, kt=kt, psg=psg):
                    return e.tensor_copy(sel[dt][:, kt * 128:kt * 128 + 128],
                                         psg[:, 0:128])
                p.v(evg, reads=[psg], writes=[sel[dt]], tag="l0fgev")
        if blk == 0:
            ctx.debug("Rr", Rr, reads=["Rr"])
            ctx.debug("R01T", R01T, reads=["R01T"])
            ctx.debug("sel0", sel[0][:], reads=[sel[0]])
            ctx.debug("nmrsel", nmrsel, reads=["ln_rows"])
        for i in range(ESB):
            es = blk * ESB + i
            ps1 = banks[6][0:64, 0:128]
            for dt in range(8):
                def mf1(e, dt=dt, i=i, st=(dt == 0)):
                    rhs = sel[dt][:].rearrange("p (k e g) -> p k e g",
                                               k=NT, e=ESB)[:, :, i, :]
                    return e.matmul(ps1, wslot[:, dt * 1024 + i * 64:
                                               dt * 1024 + i * 64 + 64],
                                    rhs, start=st, stop=False)
                p.t(mf1, reads=["Wslot", sel[dt]], writes=[banks[6]],
                    tag="l0f1")
            def mf1b(e, i=i):
                rhs = nmrsel[0:1, :].rearrange("p (k e g) -> p k e g",
                                               k=NT, e=ESB)[:, :, i, :]
                return e.matmul(ps1, csum1[0:1, i * 64:i * 64 + 64], rhs,
                                start=False, stop=True)
            p.t(mf1b, reads=[CSK1, "ln_rows"], writes=[banks[6]], tag="l0f1b")
            def evf1(e, i=i, es=es):
                return e.activation(inner[i][:], ps1, AF.Relu,
                                    bias=biasT[:, es:es + 1])
            p.s(evf1, reads=[banks[6], biasT], writes=[inner[i]], tag="l0f1ev")
        if blk == 0:
            ctx.debug("inner0", inner[0][:], reads=[inner[0]])
        f2g = ctx.carve("setB", "f2g", 0, 128, 0, 32768, BF16,
                        free=[f"B{i}" for i in range(8)])
        for half in range(2):
            for j in range(8):
                i = half * 8 + j
                es = blk * ESB + i
                def dm(e, i=i, es=es, j=j):
                    return e.dma_start(
                        out=wslot[0:64, j * 1024:j * 1024 + 1024],
                        in_=f2_d[es])
                p.op("gpsimd", dm, reads=[], writes=[wslot] + WSUBS, dma=True,
                     tag="ldf2")
            for j in range(8):
                i = half * 8 + j
                for nch in range(2):
                    psf = banks[4 + nch]
                    def mf2(e, i=i, j=j, nch=nch, psf=psf):
                        return e.matmul(
                            psf[:], inner[i][:],
                            wslot[0:64, j * 1024 + nch * 512:
                                  j * 1024 + nch * 512 + 512],
                            start=True, stop=True)
                    p.t(mf2, reads=[inner[i], "Wslot"], writes=[psf],
                        tag="l0f2")
                    def evf2(e, i=i, nch=nch, psf=psf):
                        return e.tensor_copy(
                            f2g[:, i * 1024 + nch * 512:
                                i * 1024 + nch * 512 + 512], psf[:])
                    p.v(evf2, reads=[psf], writes=["f2g"], tag="l0f2ev")
        f2g3 = f2g.rearrange("p (s d) -> p s d", s=ESB)
        for k in range(NT):
            def dmo(e, k=k):
                return e.dma_start(out=rs_kged[k], in_=f2g3[k * 8:k * 8 + 8])
            p.op("sync", dmo, reads=["f2g"], writes=["restripe"], dma=True,
                 tag="l0f_rsout")
        inter = ctx.carve("setC", "inter", 0, 128, 0, 32768, BF16,
                          free=[f"C{i}" for i in range(8)])
        idst = inter.rearrange("p (k d) -> p k d", k=NT)
        p.dma("sync", idst, rs_egkd, extra_reads=["restripe"], tag="l0f_rsin")
        if blk == 0:
            ctx.debug("inter", inter, reads=["inter"])
        for dt in range(8):
            for ktg in range(4):
                psy = banks[ktg % 2]
                for j in range(4):
                    kt = ktg * 4 + j
                    def msc(e, dt=dt, kt=kt, j=j, psy=psy):
                        return e.matmul(
                            psy[:, j * 128:j * 128 + 128],
                            inter[:, kt * 1024 + dt * 128:
                                  kt * 1024 + dt * 128 + 128],
                            R01T[:, kt * 128:kt * 128 + 128],
                            start=(j == 0), stop=(j == 3))
                    p.t(msc, reads=["inter", "R01T"], writes=[psy],
                        tag="l0fsc")
                def acc(e, dt=dt, ktg=ktg, psy=psy):
                    cs = slice(ktg * 512, ktg * 512 + 512)
                    return e.tensor_tensor(x_tiles[dt][:, cs],
                                           x_tiles[dt][:, cs], psy[:],
                                           ALU.add)
                p.v(acc, reads=[psy, x_tiles[dt]], writes=[x_tiles[dt]],
                    tag="l0facc")
        if blk + 1 < NBLK:
            ctx.retile("C")
    ctx.retile("A")
    ctx.retile("B")
    ctx.retile("C")




def emit_l1_attn(ctx, x_tiles):
    p, c = ctx.p, ctx.c
    nc = ctx.nc
    banks = ctx.banks()
    NH, HP, DS, TCH, SCH = 16, 128, 64, 512, 256

    xbf = ctx.retile("B")
    sqt = ctx.retile("C")
    ln = emit_ln_stats(ctx, x_tiles, xbf, sqt, "l1a")
    RK, negm_bf, r_bf = ln["RK"], ln["negm_bf"], ln["r_bf"]

    dr = lambda n, s: nc.dram_tensor(n, s, F32, kind="ExternalInput").ap()
    wt_d = dr("L1A_WT", [D, 4240])
    wot_d = dr("L1A_WoT", [2048, D])
    pool_specs = [
        ("l1a_convw", [128, 72], F32, "L1A_convw"),
        ("l1a_convb", [128, 18], F32, "L1A_convb"),
        ("l1a_dtb", [16, 1], F32, "L1A_dtb"),
        ("l1a_alog", [16, 1], F32, "L1A_Alog"),
        ("l1a_dpar", [1, 16], F32, "L1A_D"),
        ("l1a_normw", [128, 16], F32, "L1A_normw"),
        ("l1a_tri0", [128, 256], BF16, "hc_tri0"),
        ("l1a_tri1", [128, 256], BF16, "hc_tri1"),
    ]
    for (pn, shp, dt_, dn) in pool_specs:
        t = ctx.pool(pn, shp, dt_)
        srcd = nc.dram_tensor(dn, shp, dt_, kind="ExternalInput").ap()
        p.dma("sync", t[:], srcd, tag="l1a_ldc")
    convw = c["l1a_convw"]; convb = c["l1a_convb"]; dtb = c["l1a_dtb"]
    alog = c["l1a_alog"]; dpar = c["l1a_dpar"]; normw = c["l1a_normw"]
    tri = [c["l1a_tri0"], c["l1a_tri1"]]

    negA = ctx.pool("l1a_negA", [16, 1], F32)
    p.s(lambda e: e.activation(negA[:], alog[:], AF.Exp), reads=[alog],
        writes=[negA], tag="l1a_A")
    p.v(lambda e: e.tensor_scalar(negA[:], negA[:], -1.0, None, ALU.mult),
        reads=[negA], writes=[negA], tag="l1a_negA")
    dbc = ctx.pool("l1a_dbc", [128, 16], F32)
    p.t(lambda e: e.matmul(banks[2][:, 0:16], c["ones_row_f32"][0:1, :],
                           dpar[:], start=True, stop=True),
        reads=[dpar, c["ones_row_f32"]], writes=[banks[2]], tag="l1a_bcD")
    p.v(lambda e: e.tensor_copy(dbc[:], banks[2][:, 0:16]), reads=[banks[2]],
        writes=[dbc], tag="l1a_dbccp")
    diagD = ctx.pool("l1a_diagD", [128, 128], BF16)

    S = ctx.pool("l1a_S", [64, NH * HP], BF16)
    p.v(lambda e: e.memset(S[:], 0.0), reads=[], writes=[S], tag="l1a_S0")
    dt_em = ctx.carve("setA", "l1a_dt", 0, 16, 29312, 2048, F32)
    ones16 = ctx.carve("setA", "l1a_ones16", 0, 16, 31360, 1024, BF16)
    p.v(lambda e: e.memset(ones16[:], 1.0), reads=[], writes=[ones16])
    ccarry = ctx.pool("l1a_ccarry", [16, 1], F32)
    p.v(lambda e: e.memset(ccarry[:], 0.0), reads=[], writes=[ccarry])
    sbase = ctx.pool("l1a_sbase", [16, 1], F32)
    p.v(lambda e: e.memset(sbase[:], 0.0), reads=[], writes=[sbase])
    cch_row = ctx.pool("l1a_cchunk", [16, TCH], F32)
    s_dm = cch_row
    einit_dm = ctx.carve("arena", "l1a_einit", 0, 16, 30720, 1024, F32)
    w_dm = ctx.carve("arena", "l1a_wdm", 0, 16, 31744, 1024, F32)
    slast = ctx.pool("l1a_slast", [16, 1], F32)
    sdecay = ctx.pool("l1a_sdecay", [16, 1], F32)
    sdecay_bc = ctx.pool("l1a_sdbc", [64, 16], F32)
    srow16 = ctx.pool("l1a_srow16", [1, 16], F32)
    einit_tm = ctx.pool("l1a_einit_tm", [128, 32], F32)
    w_tm = ctx.pool("l1a_w_tm", [128, 32], F32)
    dt_tm = ctx.pool("l1a_dt_tm", [128, 32], F32)
    negs_tm = ctx.pool("l1a_negs_tm", [128, 32], F32)
    srow = ctx.carve("arena", "l1a_srow", 0, 1, 24576, 1024, F32)[0:1, :]
    gm = ctx.carve("arena", "l1a_gm", 0, 128, 25600, 1024, F32)
    gm2 = ctx.carve("arena", "l1a_gm2", 0, 128, 26624, 1024, F32)
    osc_ap = ctx.carve("arena", "l1a_osc", 0, 128, 27648, 2048, F32)
    rstd_sb = ctx.carve("arena", "l1a_rstd", 0, 128, 29696, 1024, BF16)
    bd_tm = ctx.pool("l1a_bdtm", [128, 128], BF16)
    ysc = ctx.pool("l1a_ysc", [128, 128], F32)
    diag4 = None
    halo = ctx.pool("l1a_halo", [128, 18 * 4], BF16)
    rbcs = ctx.pool("rbc_sb2", [128, 512], F32)
    rmsrow = ctx.pool("ln_rows", [1, 2560], F32)[0:1, 1024:1536]

    xbc = [ctx.carve("setA", f"xbc{i}", 0, 128, i * 1056, 1056, BF16,
                     free=[f"A{j}" for j in range(8)] if i == 0 else ())
           for i in range(16)]
    btile = ctx.carve("setA", "l1a_B", 0, 64, 16 * 1056, 1056, BF16)
    ctile = ctx.carve("setA", "l1a_C", 0, 64, 17 * 1056, 1056, BF16)
    xt_off = 18 * 1056 + 32
    wt_off = xt_off + 8192
    wts = [ctx.carve("setA", f"l1a_wt{i}", 0, 128, wt_off + i * 1024, 1024,
                     BF16) for i in range(2)]
    zch = [ctx.carve("setC", f"l1a_z{i}", 0, 128, i * 1024, 1024, BF16,
                     free=[f"C{j}" for j in range(8)] if i == 0 else ())
           for i in range(16)]
    ych = [ctx.carve("setC", f"l1a_y{i}", 0, 128, 16384 + i * 1024, 1024,
                     BF16) for i in range(16)]
    XK = [f"xbc{i}" for i in range(16)]
    ZK = [f"l1a_z{i}" for i in range(16)]
    YK = [f"l1a_y{i}" for i in range(16)]

    wslot = ctx.pool("Wslot", [128, 8 * D], BF16)
    diag4 = wslot[:, 0:512]
    DGK = "Wsub_diag4"
    WSUBS = ["Wsub_pt0", "Wsub_pt1", "Wsub_rbc", "Wsub_rrow", "Wsub_rrowbf",
             DGK]
    onesrb = c["ones_row_bf"]
    onesr = c["ones_row_f32"]
    onescol = c["ones_col_bf"]
    identb = c["ident_bf"]
    identf = c["ident_f32"]
    pstb = banks[7]

    for i, (xk, gn) in enumerate([(k, 128) for k in XK] +
                                 [("l1a_B", 64), ("l1a_C", 64)]):
        xt = (xbc + [btile, ctile])[i]
        def z0(e, xt=xt, gn=gn):
            return e.memset(xt[0:gn, 0:4], 0.0)
        p.v(z0, reads=[], writes=[xk], tag="l1a_h0")

    def wload_cols(col0, ncols):
        for kt in range(8):
            def dm(e, kt=kt, col0=col0, ncols=ncols):
                return e.dma_start(
                    out=wslot[:, kt * ncols: kt * ncols + ncols],
                    in_=wt_d[kt * 128:(kt + 1) * 128, col0:col0 + ncols])
            p.op("gpsimd", dm, reads=[], writes=[wslot] + WSUBS, dma=True,
                 tag="l1a_ldW")

    def rbc_build(tcol0, width, dst, dk):
        def bc(e, tcol0=tcol0, width=width):
            return e.matmul(banks[2][:, 0:width], onesrb[0:1, :],
                            r_bf[0:1, tcol0:tcol0 + width], start=True,
                            stop=True)
        p.t(bc, reads=[RK, onesrb], writes=[banks[2]], tag="l1a_rbc")
        def cp(e, width=width):
            return e.tensor_copy(dst[:, 0:width], banks[2][:, 0:width])
        p.v(cp, reads=[banks[2]], writes=[dk], tag="l1a_rbccp")

    def proj_group(tch, t0, col0, ncols, mtiles, evac_fn, tag):
        wload_cols(col0, ncols)
        csum, CSK = emit_colsum(ctx, wslot, f"l1a{tag}{tch}", n_out=ncols)
        for gi, (m0, mn) in enumerate(mtiles):
            ps = banks[4 + gi % 2]
            for kt in range(8):
                def mm(e, ps=ps, kt=kt, m0=m0, mn=mn, t0=t0, ncols=ncols,
                       st=(kt == 0)):
                    return e.matmul(
                        ps[0:mn, :], wslot[:, kt * ncols + m0:
                                           kt * ncols + m0 + mn],
                        xbf[kt][:, t0:t0 + 512], start=st, stop=False)
                p.t(mm, reads=["Wslot", xbf[kt]], writes=[ps], tag=tag)
            def mm1(e, ps=ps, m0=m0, mn=mn, t0=t0):
                return e.matmul(ps[0:mn, :], csum[0:1, m0:m0 + mn],
                                negm_bf[0:1, t0:t0 + 512], start=False,
                                stop=True)
            p.t(mm1, reads=[CSK, RK], writes=[ps], tag=tag + "1")
            evac_fn(gi, ps)

    NTCH = T // TCH
    for tch in range(NTCH):
        t0 = tch * TCH
        rbc_build(t0, 512, rbcs, "rbc_sb2")
        for ld in range(2):
            def evz(e_gi, ps, ld=ld):
                zt = zch[ld * 8 + e_gi]
                def ev(e, ps=ps, zt=zt):
                    return e.tensor_tensor(zt[:, 0:512], ps[0:128, :],
                                           rbcs[:], ALU.mult)
                p.v(ev, reads=[ps, "rbc_sb2"], writes=[ZK[ld * 8 + e_gi]],
                    tag="l1azev")
            proj_group(tch, t0, ld * 1024, 1024,
                       [(m * 128, 128) for m in range(8)], evz, "l1az")
        for ld in range(2):
            def evx(e_gi, ps, ld=ld):
                xt = xbc[ld * 8 + e_gi]
                def ev(e, ps=ps, xt=xt):
                    return e.tensor_tensor(xt[:, 4:516], ps[0:128, :],
                                           rbcs[:], ALU.mult)
                p.v(ev, reads=[ps, "rbc_sb2"], writes=[XK[ld * 8 + e_gi]],
                    tag="l1axev")
            proj_group(tch, t0, 2048 + ld * 1024, 1024,
                       [(m * 128, 128) for m in range(8)], evx, "l1ax")
        def evbc(e_gi, ps):
            dst, dk, gn = [(btile, "l1a_B", 64), (ctile, "l1a_C", 64),
                           (None, None, 16)][e_gi]
            if e_gi < 2:
                def ev(e, ps=ps, dst=dst, gn=gn):
                    return e.tensor_tensor(dst[0:gn, 4:516], ps[0:gn, :],
                                           rbcs[0:gn, :], ALU.mult)
                p.v(ev, reads=[ps, "rbc_sb2"], writes=[dk], tag="l1abcev")
            else:
                def ev(e, ps=ps, t0=t0):
                    return e.tensor_tensor(dt_em[:, 0:512],
                                           ps[0:16, :], rbcs[0:16, :],
                                           ALU.mult)
                p.v(ev, reads=[ps, "rbc_sb2"], writes=[dt_em], tag="l1adtev")
        proj_group(tch, t0, 4096, 144, [(0, 64), (64, 64), (128, 16)],
                   evbc, "l1abc")
        def sp1(e, t0=t0):
            return e.activation(dt_em[:, 0:512], dt_em[:, 0:512],
                                AF.Exp, bias=dtb[:])
        p.s(sp1, reads=[dt_em, dtb], writes=[dt_em], tag="l1asp1")
        def sp2(e, t0=t0):
            return e.activation(dt_em[:, 0:512], dt_em[:, 0:512],
                                AF.Ln, bias=1.0)
        p.s(sp2, reads=[dt_em], writes=[dt_em], tag="l1asp2")

        tiles18 = [(xbc[i], XK[i], 128, i) for i in range(16)] + \
            [(btile, "l1a_B", 64, 16), (ctile, "l1a_C", 64, 17)]
        for (xt, xk, gn, gi) in tiles18:
            for tap in range(4):
                def mkdg(e, gn=gn, gi=gi, tap=tap):
                    return e.tensor_scalar(
                        diag4[0:gn, tap * 128:tap * 128 + gn],
                        identb[0:gn, 0:gn],
                        convw[0:gn, gi * 4 + tap:gi * 4 + tap + 1], None,
                        ALU.mult)
                p.v(mkdg, reads=[identb, convw], writes=[DGK], tag="l1adg")
            psc = banks[4 + gi % 2]
            for tap in range(4):
                def mmc(e, xt=xt, gn=gn, tap=tap, psc=psc, st=(tap == 0),
                        sp=(tap == 3)):
                    return e.matmul(psc[0:gn, :],
                                    diag4[0:gn, tap * 128:tap * 128 + gn],
                                    xt[0:gn, tap + 1:tap + 513], start=st,
                                    stop=sp)
                p.t(mmc, reads=[DGK, xk], writes=[psc], tag="l1acv")
            if tch + 1 < NTCH:
                def sv(e, xt=xt, gn=gn, gi=gi):
                    return e.tensor_copy(halo[0:gn, gi * 4:gi * 4 + 4],
                                         xt[0:gn, 512:516])
                p.v(sv, reads=[xk], writes=["l1a_halo"], tag="l1ahalo")
            def sl(e, xt=xt, gn=gn, gi=gi, psc=psc):
                return e.activation(xt[0:gn, 4:516], psc[0:gn, :], AF.Silu,
                                    bias=convb[0:gn, gi:gi + 1])
            p.s(sl, reads=[psc, convb], writes=[xk], tag="l1asilu")
            if tch + 1 < NTCH:
                def rh(e, xt=xt, gn=gn, gi=gi):
                    return e.tensor_copy(xt[0:gn, 0:4],
                                         halo[0:gn, gi * 4:gi * 4 + 4])
                p.v(rh, reads=["l1a_halo"], writes=[xk], tag="l1ahl2")

        if tch == 0:
            ctx.debug("xbc0", xbc[0][:, 0:516], reads=[XK[0]])
            ctx.debug("btile", btile[0:64, 0:516], reads=["l1a_B"])
            ctx.debug("dtem", dt_em[:, 0:512], reads=[dt_em])
            ctx.debug("z0", zch[0][:, 0:512], reads=[ZK[0]])
        def scn(e, t0=t0):
            return e.tensor_tensor_scan(cch_row[:], ones16[:],
                                        dt_em[:, 0:512], ccarry[:],
                                        ALU.mult, ALU.add)
        p.v(scn, reads=[ones16, dt_em, ccarry], writes=[cch_row], tag="l1asc")
        p.v(lambda e: e.tensor_copy(ccarry[:], cch_row[:, 511:512]),
            reads=[cch_row], writes=[ccarry], tag="l1accr")
        p.v(lambda e: e.tensor_scalar(s_dm[:], cch_row[:], negA[:], None,
                                      ALU.mult), reads=[cch_row, negA],
            writes=[cch_row], tag="l1asdm")

        if tch == 0:
            ctx.debug("sdm", s_dm[:], reads=[cch_row])
        for sci in range(2):
            c0 = sci * SCH
            p.v(lambda e, c0=c0: e.tensor_scalar(
                einit_dm[:], s_dm[:, c0:c0 + SCH], sbase[:], None,
                ALU.subtract), reads=[cch_row, sbase], writes=[einit_dm],
                tag="l1aei0")
            p.s(lambda e: e.activation(einit_dm[:], einit_dm[:], AF.Exp),
                reads=[einit_dm], writes=[einit_dm], tag="l1aei1")
            p.v(lambda e, c0=c0: e.tensor_copy(
                slast[:], s_dm[:, c0 + SCH - 1:c0 + SCH]),
                reads=[cch_row], writes=[slast], tag="l1asl")
            p.v(lambda e, c0=c0: e.tensor_scalar(
                w_dm[:], s_dm[:, c0:c0 + SCH], slast[:], None, ALU.subtract),
                reads=[cch_row, slast], writes=[w_dm], tag="l1aw0")
            p.s(lambda e: e.activation(w_dm[:], w_dm[:], AF.Exp, scale=-1.0),
                reads=[w_dm], writes=[w_dm], tag="l1aw1")
            def wmul(e, c0=c0, t0=t0):
                return e.tensor_tensor(w_dm[:], w_dm[:],
                                       dt_em[:, c0:c0 + SCH],
                                       ALU.mult)
            p.v(wmul, reads=[w_dm, dt_em], writes=[w_dm], tag="l1aw2")
            p.v(lambda e: e.tensor_scalar(sdecay[:], slast[:], sbase[:], None,
                                          ALU.subtract),
                reads=[slast, sbase], writes=[sdecay], tag="l1asd0")
            p.s(lambda e: e.activation(sdecay[:], sdecay[:], AF.Exp),
                reads=[sdecay], writes=[sdecay], tag="l1asd1")
            p.v(lambda e: e.tensor_copy(sbase[:], slast[:]),
                reads=[slast], writes=[sbase], tag="l1asb")
            p.t(lambda e: e.transpose(banks[2][0:1, 0:16], sdecay[:],
                                      identf[0:16, 0:16]),
                reads=[sdecay, identf], writes=[banks[2]], tag="l1atsd")
            p.v(lambda e: e.tensor_copy(srow16[:], banks[2][0:1, 0:16]),
                reads=[banks[2]], writes=[srow16], tag="l1atsdc")
            p.t(lambda e: e.matmul(banks[2][0:64, 0:16], onesr[0:1, 0:64],
                                   srow16[:], start=True, stop=True),
                reads=[srow16, onesr], writes=[banks[2]], tag="l1absd")
            p.v(lambda e: e.tensor_copy(sdecay_bc[:], banks[2][0:64, 0:16]),
                reads=[banks[2]], writes=[sdecay_bc], tag="l1absdc")
            trows = [(einit_dm[:], "l1a_einit", einit_tm, "l1a_einit_tm"),
                     (w_dm[:], "l1a_wdm", w_tm, "l1a_w_tm"),
                     (s_dm[:, c0:c0 + SCH], "l1a_sdm", negs_tm,
                      "l1a_negs_tm"),
                     (dt_em[:, c0:c0 + SCH], "l1a_dt", dt_tm,
                      "l1a_dt_tm")]
            for (row, rk, dst, dk) in trows:
                for jt in range(2):
                    def tp(e, row=row, jt=jt):
                        return e.transpose(banks[2][:, 0:16],
                                           row[0:16, jt * 128:jt * 128 + 128],
                                           identf[0:16, 0:16])
                    p.t(tp, reads=[rk if rk != "l1a_sdm" else "l1a_cchunk", identf],
                        writes=[banks[2]],
                        tag="l1atpr")
                    def ev(e, dst=dst, jt=jt):
                        return e.tensor_copy(
                            dst[:, jt * 16:jt * 16 + 16],
                            banks[2][:, 0:16])
                    p.v(ev, reads=[banks[2]], writes=[dk], tag="l1atprev")
            s_tm = ctx.pool("l1a_s_tm", [128, 32], F32)
            p.v(lambda e: e.tensor_copy(s_tm[:], negs_tm[:]),
                reads=["l1a_negs_tm"], writes=["l1a_s_tm"], tag="l1astm")
            p.v(lambda e: e.tensor_scalar(negs_tm[:], negs_tm[:], -1.0, None,
                                          ALU.mult),
                reads=["l1a_negs_tm"], writes=["l1a_negs_tm"], tag="l1aneg")
            xtm = [ctx.carve("setA", f"l1a_xtm{i}", 0, 128,
                             xt_off + i * 4096, 4096, BF16,
                             free=("l1a_ygsq",) if i == 0 else ())
                   for i in range(2)]
            for pt in range(16):
                for jt in range(2):
                    def tp(e, pt=pt, jt=jt, c0=c0):
                        return e.transpose(
                            pstb[:, jt * 128:jt * 128 + 128],
                            xbc[pt][:, 4 + c0 + jt * 128:
                                     4 + c0 + jt * 128 + 128], identb[:])
                    p.t(tp, reads=[XK[pt], identb], writes=[banks[7]],
                        tag="l1axtp")
                for jt in range(2):
                    def ev(e, pt=pt, jt=jt):
                        return e.tensor_copy(
                            xtm[jt][:, pt * 128:pt * 128 + 128],
                            pstb[:, jt * 128:jt * 128 + 128])
                    p.v(ev, reads=[banks[7]], writes=[f"l1a_xtm{jt}"],
                        tag="l1axtpev")
            for jt in range(2):
                def tp(e, jt=jt, c0=c0):
                    return e.transpose(pstb[:, 256:256 + 64],
                                       btile[0:64, 4 + c0 + jt * 128:
                                             4 + c0 + jt * 128 + 128],
                                       identb[0:64, 0:64])
                p.t(tp, reads=["l1a_B", identb], writes=[banks[7]],
                    tag="l1abtp")
                def ev(e, jt=jt):
                    return e.tensor_copy(
                        bd_tm[:, jt * 64:jt * 64 + 64],
                        pstb[:, 256:256 + 64])
                p.v(ev, reads=[banks[7]], writes=["l1a_bdtm"], tag="l1abtpev")

            for jt in range(2):
                psg = banks[jt]
                def mg(e, jt=jt, c0=c0, psg=psg):
                    return e.matmul(
                        psg[:, 0:SCH],
                        btile[0:64, 4 + c0 + jt * 128:4 + c0 + jt * 128 + 128],
                        ctile[0:64, 4 + c0:4 + c0 + SCH],
                        start=True, stop=True)
                p.t(mg, reads=["l1a_B", "l1a_C"], writes=[psg], tag="l1aG")
                def gmask(e, jt=jt, psg=psg):
                    dst = gm if jt == 0 else gm2
                    return e.tensor_tensor(dst[:], psg[:, 0:SCH], tri[jt][:],
                                           ALU.mult)
                p.v(gmask, reads=[psg, "l1a_tri0", "l1a_tri1"],
                    writes=[gm if jt == 0 else gm2], tag="l1aGm")
            for h in range(NH):
                for jt2 in range(2):
                    def tpc(e, h=h, jt2=jt2):
                        col = negs_tm[:, jt2 * 16 + h: jt2 * 16 + h + 1]
                        return e.transpose(banks[2][0:1, 0:128], col,
                                           identf[:])
                    p.t(tpc, reads=["l1a_negs_tm", identf], writes=[banks[2]],
                        tag="l1asrt")
                    def evc(e, jt2=jt2):
                        return e.tensor_scalar(
                            srow[0:1, jt2 * 128:jt2 * 128 + 128],
                            banks[2][0:1, 0:128], -1.0, None, ALU.mult)
                    p.v(evc, reads=[banks[2]], writes=["l1a_srow"],
                        tag="l1asrtev")
                for jt in range(2):
                    psd = banks[3]
                    def mbc(e, psd=psd):
                        return e.matmul(psd[:, 0:SCH], onesr[0:1, :],
                                        srow[0:1, :], start=True, stop=True)
                    p.t(mbc, reads=["l1a_srow", onesr], writes=[psd],
                        tag="l1aEb")
                    negs_col = negs_tm[:, jt * 16 + h: jt * 16 + h + 1]
                    s_col = ctx.pool("l1a_s_tm", [128, 32], F32)[
                        :, jt * 16 + h: jt * 16 + h + 1]
                    dtcol = dt_tm[:, jt * 16 + h: jt * 16 + h + 1]
                    gmt = gm if jt == 0 else gm2
                    def clmp(e, psd=psd, s_col=s_col):
                        return e.tensor_scalar(psd[:, 0:SCH], psd[:, 0:SCH],
                                               s_col, None, ALU.min)
                    p.v(clmp, reads=[psd, "l1a_s_tm"], writes=[psd],
                        tag="l1aclmp")
                    def eexp(e, psd=psd, negs_col=negs_col):
                        return e.activation(banks[3][:, 0:SCH],
                                            psd[:, 0:SCH], AF.Exp,
                                            bias=negs_col)
                    p.s(eexp, reads=[banks[3], "l1a_negs_tm"],
                        writes=[banks[3]], tag="l1aEe")
                    def wtb(e, gmt=gmt, dtcol=dtcol, jt=jt):
                        return e.scalar_tensor_tensor(
                            wts[jt][:, 0:SCH], gmt[:], dtcol,
                            banks[3][:, 0:SCH], ALU.mult, ALU.mult)
                    p.v(wtb, reads=[gm if jt == 0 else gm2, "l1a_dt_tm",
                                    banks[3]],
                        writes=[f"l1a_wt{jt}"], tag="l1aWt")
                def mkd(e, h=h):
                    return e.tensor_scalar(diagD[:], c["ident_bf"][:],
                                           dbc[:, h:h + 1], None, ALU.mult)
                p.v(mkd, reads=[c["ident_bf"], dbc], writes=[diagD],
                    tag="l1a_diag")
                for it in range(2):
                    ps4 = banks[4]
                    for jt in range(2):
                        def my(e, h=h, it=it, jt=jt, ps4=ps4,
                               st=(jt == 0)):
                            return e.matmul(
                                ps4[:, it * 128:it * 128 + 128],
                                wts[jt][:, it * 128:it * 128 + 128],
                                xtm[jt][:, h * 128:h * 128 + 128],
                                start=st, stop=False)
                        p.t(my, reads=[f"l1a_wt{jt}", f"l1a_xtm{jt}"],
                            writes=[ps4], tag="l1aYi")
                    def mdx(e, h=h, it=it, ps4=ps4):
                        return e.matmul(ps4[:, it * 128:it * 128 + 128],
                                        diagD[:],
                                        xtm[it][:, h * 128:h * 128 + 128],
                                        start=False, stop=True)
                    p.t(mdx, reads=[diagD, f"l1a_xtm{it}"], writes=[ps4],
                        tag="l1aDx")
                    ps5 = banks[5]
                    def mi(e, h=h, it=it, c0=c0, ps5=ps5):
                        return e.matmul(
                            ps5[:, it * 128:it * 128 + 128],
                            ctile[0:64, 4 + c0 + it * 128:
                                  4 + c0 + it * 128 + 128],
                            S[:, h * 128:h * 128 + 128], start=True,
                            stop=True)
                    p.t(mi, reads=["l1a_C", S], writes=[ps5], tag="l1aYs")
                for it in range(2):
                    einit_col = einit_tm[:, it * 16 + h: it * 16 + h + 1]
                    def cmb(e, it=it, einit_col=einit_col):
                        return e.activation(ysc[:],
                                            banks[5][:, it * 128:
                                                     it * 128 + 128],
                                            AF.Copy, scale=einit_col)
                    p.s(cmb, reads=[banks[5], "l1a_einit_tm"], writes=[ysc],
                        tag="l1aYc1")
                    def cmb2(e, it=it):
                        return e.tensor_tensor(ysc[:], ysc[:],
                                               banks[4][:, it * 128:
                                                        it * 128 + 128],
                                               ALU.add)
                    p.v(cmb2, reads=[ysc, banks[4]], writes=[ysc],
                        tag="l1aYc2")
                    ysb = ctx.carve("arena", "l1a_ysb", 0, 128, 32768,
                                    256, BF16)
                    p.v(lambda e: e.tensor_copy(ysb[:], ysc[:]),
                        reads=[ysc], writes=[ysb], tag="l1aYcb")
                    def typ(e):
                        return e.transpose(pstb[:, 512:512 + 128], ysb[:],
                                           identb[:])
                    p.t(typ, reads=[ysb, identb], writes=[banks[7]],
                        tag="l1aYt")
                    def yev(e, h=h, it=it, c0=c0):
                        return e.tensor_copy(
                            ych[h][:, c0 + it * 128:c0 + it * 128 + 128],
                            pstb[:, 512:512 + 128])
                    p.v(yev, reads=[banks[7]], writes=[YK[h]], tag="l1aYev")
                bdecay = ctx.carve("arena", "l1a_bdecay", 0, 128, 33024,
                                   256, BF16)
                for jt in range(2):
                    wcol = w_tm[:, jt * 16 + h: jt * 16 + h + 1]
                    def bd(e, jt=jt, wcol=wcol):
                        src = bd_tm[:, jt * 64:jt * 64 + 64]
                        dstv = bdecay[:, jt * 64:jt * 64 + 64]
                        return e.tensor_scalar(dstv, src, wcol, None,
                                               ALU.mult)
                    p.v(bd, reads=["l1a_bdtm", "l1a_w_tm"],
                        writes=["l1a_bdecay"], tag="l1aBd")
                ps6 = banks[6]
                for jt in range(2):
                    def msa(e, h=h, jt=jt, ps6=ps6, st=(jt == 0),
                            sp=(jt == 1)):
                        return e.matmul(
                            ps6[0:64, 0:128],
                            bdecay[:, jt * 64:jt * 64 + 64],
                            xtm[jt][:, h * 128:h * 128 + 128],
                            start=st, stop=sp)
                    p.t(msa, reads=["l1a_bdecay", f"l1a_xtm{jt}"],
                        writes=[ps6], tag="l1aSa")
                def sup(e, h=h, ps6=ps6):
                    return e.scalar_tensor_tensor(
                        S[:, h * 128:h * 128 + 128],
                        S[:, h * 128:h * 128 + 128],
                        sdecay_bc[:, h:h + 1], ps6[0:64, 0:128],
                        ALU.mult, ALU.add)
                p.v(sup, reads=[S, sdecay_bc, ps6], writes=[S], tag="l1aSu")

        if tch == 0:
            ctx.debug("y0", ych[0][:, 0:512], reads=[YK[0]])
            ctx.debug("y5", ych[5][:, 0:512], reads=[YK[5]])
        for i in range(16):
            def sz(e, i=i):
                return e.activation(zch[i][:, 0:512], zch[i][:, 0:512],
                                    AF.Silu)
            p.s(sz, reads=[ZK[i]], writes=[ZK[i]], tag="l1asz")
            def yg(e, i=i):
                return e.tensor_tensor(ych[i][:, 0:512], ych[i][:, 0:512],
                                       zch[i][:, 0:512], ALU.mult)
            p.v(yg, reads=[YK[i], ZK[i]], writes=[YK[i]], tag="l1ayg")
        ps0 = banks[0][0:1, :]
        ygsq = ctx.carve("setA", "l1a_ygsq", 0, 128, xt_off, 1024, BF16,
                         free=("l1a_xtm0",))
        for i in range(16):
            def sq(e, i=i):
                return e.activation(ygsq[:], ych[i][:, 0:512], AF.Square)
            p.s(sq, reads=[YK[i]], writes=["l1a_ygsq"], tag="l1aq")
            def mq(e, i=i, st=(i == 0), sp=(i == 15)):
                return e.matmul(ps0, onescol[:], ygsq[:], start=st, stop=sp)
            p.t(mq, reads=["l1a_ygsq", onescol], writes=[banks[0]],
                tag="l1amq")
        p.v(lambda e: e.tensor_scalar(rmsrow, ps0, 1.0 / 2048.0,
                                      None, ALU.mult),
            reads=[banks[0]], writes=["ln_rows"], tag="l1arms0")
        p.s(lambda e: e.activation(rmsrow, rmsrow, AF.Ln, bias=1e-5),
            reads=["ln_rows"], writes=["ln_rows"], tag="l1arms1")
        p.s(lambda e: e.activation(rmsrow, rmsrow, AF.Exp, scale=-0.5),
            reads=["ln_rows"], writes=["ln_rows"], tag="l1arms2")
        for half in range(1):
            def bcr(e):
                return e.matmul(banks[2][:, 0:512], onesr[0:1, :],
                                rmsrow, start=True, stop=True)
            p.t(bcr, reads=["ln_rows", onesr], writes=[banks[2]], tag="l1abcr")
            p.v(lambda e: e.tensor_copy(rstd_sb[:, 0:512],
                                        banks[2][:, 0:512]),
                reads=[banks[2]], writes=[rstd_sb], tag="l1abcrc")
        for i in range(16):
            def nw(e, i=i):
                return e.tensor_scalar(ych[i][:, 0:512], ych[i][:, 0:512],
                                       normw[:, i:i + 1], None, ALU.mult)
            p.v(nw, reads=[YK[i], normw], writes=[YK[i]], tag="l1anw")
        osc = osc_ap
        for pass_ in range(2):
            for ld in range(2):
                for kt in range(8):
                    def dm(e, kt=kt, ld=ld):
                        return e.dma_start(
                            out=wslot[:, kt * 1024:(kt + 1) * 1024],
                            in_=wot_d[ld * 1024 + kt * 128:
                                      ld * 1024 + (kt + 1) * 128, :])
                    p.op("gpsimd", dm, reads=[], writes=[wslot] + WSUBS,
                         dma=True, tag="l1a_ldWo")
                for mi in range(4):
                    mt = pass_ * 4 + mi
                    psm = banks[mi]
                    for kt in range(8):
                        ych_i = ld * 8 + kt
                        def mo(e, mt=mt, kt=kt, ych_i=ych_i, psm=psm,
                               st=(ld == 0 and kt == 0)):
                            return e.matmul(
                                psm[:],
                                wslot[:, kt * 1024 + mt * 128:
                                      kt * 1024 + mt * 128 + 128],
                                ych[ych_i][:, 0:512], start=st, stop=False)
                        p.t(mo, reads=["Wslot", YK[ych_i]], writes=[psm],
                            tag="l1amo")
            for mi in range(4):
                mt = pass_ * 4 + mi
                psm = banks[mi]
                def ev1(e, psm=psm):
                    return e.tensor_tensor(osc[:], psm[:], rstd_sb[:, 0:512],
                                           ALU.mult)
                p.v(ev1, reads=[psm, rstd_sb], writes=[osc], tag="l1aoev1")
                def ev2(e, mt=mt, t0=t0):
                    return e.tensor_tensor(x_tiles[mt][:, t0:t0 + 512],
                                           x_tiles[mt][:, t0:t0 + 512],
                                           osc[:], ALU.add)
                p.v(ev2, reads=[osc, x_tiles[mt]], writes=[x_tiles[mt]],
                    tag="l1aoev2")
    ctx.retile("A")
    ctx.retile("B")
    ctx.retile("C")




def emit_l1_ffn(ctx, x_tiles):
    p, c = ctx.p, ctx.c
    nc = ctx.nc
    banks = ctx.banks()
    NHEADS, DH, WIN = 8, 64, 512

    dr32 = lambda n, s: nc.dram_tensor(n, s, F32, kind="ExternalInput").ap()
    drbf = lambda n, s: nc.dram_tensor(n, s, BF16, kind="ExternalInput").ap()
    wqkv_d = dr32("L1F_wqkv", [2, D, 1536])
    wo_d = dr32("L1F_wo", [2, 512, D])
    w1a_d = dr32("L1F_w1a", [2, D, 3072])
    w1g_d = dr32("L1F_w1g", [2, D, 3072])
    w2_d = dr32("L1F_w2", [2, 3072, D])
    qs_d = dr32("L1F_qs", [128, 8])
    lng_d = dr32("L1F_lng", [128, 16])
    lnb_d = dr32("L1F_lnb", [128, 16])
    lnbb_d = drbf("L1F_lnbb", [128, 16])
    lngb_d = drbf("L1F_lngb", [128, 16])
    b1a_d = drbf("L1F_b1a", [2, 1, 3072])
    b1g_d = drbf("L1F_b1g", [2, 1, 3072])
    b2_d = dr32("L1F_b2", [128, 16])
    rotp_d = drbf("hc_rotp", [128, 128])
    triA_d = drbf("hc_triA", [128, 128])
    triB_d = drbf("hc_triB", [128, 128])
    csq_d = drbf("hc_csq", [128, T])
    ssq_d = drbf("hc_ssq", [128, T])
    csk_d = drbf("hc_csk", [128, T])
    ssk_d = drbf("hc_ssk", [128, T])

    qsc = ctx.pool("l1f_qs", [128, 8], F32)
    lng = ctx.pool("l1f_lng", [128, 16], F32)
    lnb = ctx.pool("l1f_lnb", [128, 16], F32)
    lnbb = ctx.pool("l1f_lnbb", [128, 16], BF16)
    lngb = ctx.pool("l1f_lngb", [128, 16], BF16)
    b2c = ctx.pool("l1f_b2", [128, 16], F32)
    rotp = ctx.pool("l1f_rotp", [128, 128], BF16)
    triA = ctx.pool("l1f_triA", [128, 128], BF16)
    triB = ctx.pool("l1f_triB", [128, 128], BF16)
    b1row = ctx.pool("l1f_b1", [1, 1024], BF16)
    bo64 = ctx.pool("l1f_bo64", [128, 2], BF16)
    repl2 = ctx.pool("l1f_repl2", [2, 128], BF16)
    bo64_d = drbf("hc_bo64", [128, 2])
    repl2_d = drbf("hc_repl2", [2, 128])
    for dst, srcd in ((qsc, qs_d), (lng, lng_d), (lnb, lnb_d), (lnbb, lnbb_d),
                      (lngb, lngb_d),
                      (b2c, b2_d), (rotp, rotp_d), (triA, triA_d),
                      (triB, triB_d), (bo64, bo64_d), (repl2, repl2_d)):
        p.dma("sync", dst[:], srcd, tag="l1f_ldc")

    wslot = ctx.pool("Wslot", [128, 8 * D], BF16)
    WSUBS = ["Wsub_pt0", "Wsub_pt1", "Wsub_rbc", "Wsub_rrow", "Wsub_rrowbf",
             "Wsub_diag4"]
    onesrb = c["ones_row_bf"]
    identb = c["ident_bf"]
    pstb = banks[7]
    rows_t = ctx.pool("ln_rows", [1, 2560], F32)

    tabA = ctx.carve("arena", "l1f_tabA", 0, 128, 24576, 4096, BF16,
                     free=("l1a_srow", "l1a_gm", "l1a_gm2", "l1a_osc",
                           "l1a_rstd", "l1a_einit", "l1a_wdm", "l1a_ysb",
                           "l1a_bdecay"))
    tabB = ctx.carve("arena", "l1f_tabB", 0, 128, 28672, 4096, BF16)

    ctx.debug("l1fxin0", x_tiles[0][:], reads=[x_tiles[0]])
    spill2 = nc.dram_tensor("spill2", [8, 128, T], F32, kind="Internal").ap()
    xbf0 = ctx.retile("B")
    sqt0 = ctx.retile("A")
    ln0 = emit_ln_stats(ctx, x_tiles, xbf0, sqt0, "l1fpre")
    RK0, negm0, r0 = ln0["RK"], ln0["negm_bf"], ln0["r_bf"]
    for i in range(8):
        p.dma("sync", spill2[i], x_tiles[i][:], tag="l1f_spill")
    for cch in range(4):
        cs = slice(cch * 512, cch * 512 + 512)
        def bc1(e, cs=cs):
            return e.matmul(banks[2][:], onesrb[0:1, :], r0[0:1, cs],
                            start=True, stop=True)
        p.t(bc1, reads=[RK0, onesrb], writes=[banks[2]], tag="l1fprbc")
        def bc2(e, cs=cs):
            return e.matmul(banks[3][:], onesrb[0:1, :], negm0[0:1, cs],
                            start=True, stop=True)
        p.t(bc2, reads=[RK0, onesrb], writes=[banks[3]], tag="l1fpmbc")
        for i in range(8):
            def n1(e, xt=x_tiles[i], cs=cs):
                return e.tensor_tensor(xt[:, cs], xt[:, cs], banks[3][:],
                                       ALU.add)
            p.v(n1, reads=[x_tiles[i], banks[3]], writes=[x_tiles[i]],
                tag="l1fn1")
            def n2(e, xt=x_tiles[i], cs=cs):
                return e.tensor_tensor(xt[:, cs], xt[:, cs], banks[2][:],
                                       ALU.mult)
            p.v(n2, reads=[x_tiles[i], banks[2]], writes=[x_tiles[i]],
                tag="l1fn2")
    ctx.debug("l1fu0", x_tiles[0][:], reads=[x_tiles[0]])

    for layer in range(2):
        xbf = ctx.retile("B")
        XBK = [f"B{i}" for i in range(8)]
        for i in range(8):
            if i < 4:
                def cp(e, a=x_tiles[i], b=xbf[i]):
                    return e.copy(b[:], a[:])
                p.s(cp, reads=[x_tiles[i]], writes=[XBK[i]], tag="l1fcast")
            else:
                def cp(e, a=x_tiles[i], b=xbf[i]):
                    return e.copy(b[:, 1:T], a[:, 0:T - 1])
                p.s(cp, reads=[x_tiles[i]], writes=[XBK[i]], tag="l1fcasts")
                def z0(e, b=xbf[i]):
                    return e.memset(b[:, 0:1], 0.0)
                p.v(z0, reads=[], writes=[XBK[i]], tag="l1fz0")

        setA = ctx.retile("A")
        q_bf = setA[0:4]
        k_bf = setA[4:8]
        QK = [f"A{i}" for i in range(4)]
        KK = [f"A{i}" for i in range(4, 8)]
        setC = ctx.retile("C")
        v_ext = [ctx.carve("setC", f"l1f_vx{tt}", 0, 128, tt * 1056, 1040,
                           BF16, free=[f"C{j}" for j in range(8)]
                           if tt == 0 else ()) for tt in range(NT)]
        VXK = [f"l1f_vx{tt}" for tt in range(NT)]
        qf32 = ctx.carve("setC", "l1f_qf32", 0, 128, 17408, 8192, F32)
        sqbf = ctx.carve("setC", "l1f_sqbf", 0, 128, 25600, 4096, BF16)
        rrsb = ctx.carve("setC", "l1f_rrsb", 0, 2, 29696, 1024, BF16)
        scr3 = ctx.carve("setC", "l1f_scr3", 0, 128, 30720, 2048, F32)[:, 0:512]
        for kt in range(8):
            def dm(e, kt=kt, layer=layer):
                return e.dma_start(out=wslot[:, kt * 1024:(kt + 1) * 1024],
                                   in_=wqkv_d[layer, kt * 128:(kt + 1) * 128,
                                              0:1024])
            p.op("gpsimd", dm, reads=[], writes=[wslot] + WSUBS, dma=True,
                 tag="l1f_ldqkv")
        for (dst, DK, base, tab_cs, tab_ss, qcol, qrcol) in (
                (q_bf, QK, 0, csq_d, ssq_d, qsc[:, layer * 4 + 0:layer * 4 + 1]
                 if False else qsc[:, 0 + layer * 4:1 + layer * 4],
                 qsc[:, 1 + layer * 4:2 + layer * 4]),
                (k_bf, KK, 512, csk_d, ssk_d,
                 qsc[:, 2 + layer * 4:3 + layer * 4],
                 qsc[:, 3 + layer * 4:4 + layer * 4])):
            p.dma("sync", tabA[:], tab_cs, tag="l1f_ldtab")
            p.dma("sync", tabB[:], tab_ss, tag="l1f_ldtab")
            def f1(e, qcol=qcol):
                return e.tensor_scalar(tabA[:], tabA[:], qcol, None, ALU.mult)
            p.v(f1, reads=["l1f_tabA", qsc], writes=["l1f_tabA"], tag="l1ftf")
            def f2(e, qrcol=qrcol):
                return e.tensor_scalar(tabB[:], tabB[:], qrcol, None,
                                       ALU.mult)
            p.v(f2, reads=["l1f_tabB", qsc], writes=["l1f_tabB"], tag="l1ftf2")
            for mt in range(4):
                for cch in range(4):
                    ps = banks[4 + cch % 2]
                    for kt in range(8):
                        def mm(e, ps=ps, kt=kt, mt=mt, cch=cch, base=base,
                               st=(kt == 0), sp=(kt == 7)):
                            return e.matmul(
                                ps[:],
                                wslot[:, kt * 1024 + base + mt * 128:
                                      kt * 1024 + base + mt * 128 + 128],
                                xbf[kt][:, cch * 512:cch * 512 + 512],
                                start=st, stop=sp)
                        p.t(mm, reads=["Wslot", XBK[kt]], writes=[ps],
                            tag="l1fqk")
                    def ev(e, ps=ps, cch=cch):
                        return e.tensor_copy(qf32[:, cch * 512:cch * 512 + 512],
                                             ps[:])
                    p.v(ev, reads=[ps], writes=["l1f_qf32"], tag="l1fqkev")
                def sq(e):
                    return e.activation(sqbf[:], qf32[:], AF.Square)
                p.s(sq, reads=["l1f_qf32"], writes=["l1f_sqbf"], tag="l1fsq")
                for cch in range(4):
                    ps2a = banks[2][0:2, :]
                    def mm(e, cch=cch, ps2a=ps2a):
                        return e.matmul(
                            ps2a, bo64[:],
                            sqbf[:, cch * 512:cch * 512 + 512],
                            start=True, stop=True)
                    p.t(mm, reads=["l1f_sqbf", bo64], writes=[banks[2]],
                        tag="l1fss")
                    def ln1(e, ps2a=ps2a):
                        return e.activation(ps2a, ps2a, AF.Ln)
                    p.s(ln1, reads=[banks[2]], writes=[banks[2]], tag="l1fl2a")
                    def ex1(e, ps2a=ps2a):
                        return e.activation(ps2a, ps2a, AF.Exp, scale=-0.5)
                    p.s(ex1, reads=[banks[2]], writes=[banks[2]], tag="l1fl2b")
                    def mn(e, cch=cch, ps2a=ps2a):
                        return e.tensor_scalar(rrsb[0:2, 0:512],
                                               ps2a, 1e12, None, ALU.min)
                    p.v(mn, reads=[banks[2]], writes=["l1f_rrsb"],
                        tag="l1fl2c")
                    def xp(e):
                        return e.matmul(banks[2][:], repl2[:],
                                        rrsb[0:2, 0:512], start=True,
                                        stop=True)
                    p.t(xp, reads=["l1f_rrsb", repl2], writes=[banks[2]],
                        tag="l1fxp")
                    def cst(e, cch=cch):
                        return e.copy(sqbf[:, cch * 512:cch * 512 + 512],
                                      qf32[:, cch * 512:cch * 512 + 512])
                    p.s(cst, reads=["l1f_qf32"], writes=["l1f_sqbf"],
                        tag="l1fqb")
                    def rmm(e, cch=cch):
                        return e.matmul(banks[3][:], rotp[:],
                                        sqbf[:, cch * 512:cch * 512 + 512],
                                        start=True, stop=True)
                    p.t(rmm, reads=["l1f_sqbf", rotp], writes=[banks[3]],
                        tag="l1frot")
                    def c1(e, mt=mt, cch=cch):
                        return e.tensor_tensor(
                            scr3[:], qf32[:, cch * 512:cch * 512 + 512],
                            tabA[:, mt * 512 + cch * 512:
                                 mt * 512 + cch * 512 + 512]
                            if False else
                            tabA[:, cch * 512:cch * 512 + 512], ALU.mult)
                    p.v(c1, reads=["l1f_qf32", "l1f_tabA"], writes=["l1f_scr3"],
                        tag="l1fc1")
                    def c2(e, cch=cch):
                        return e.scalar_tensor_tensor(
                            scr3[:], banks[3][:], 1.0, scr3[:], ALU.mult,
                            ALU.add)
                    def c2a(e, cch=cch):
                        return e.tensor_tensor(
                            banks[3][:], banks[3][:],
                            tabB[:, cch * 512:cch * 512 + 512], ALU.mult)
                    p.v(c2a, reads=[banks[3], "l1f_tabB"], writes=[banks[3]],
                        tag="l1fc2a")
                    def c2b(e):
                        return e.tensor_tensor(scr3[:], scr3[:], banks[3][:],
                                               ALU.add)
                    p.v(c2b, reads=["l1f_scr3", banks[3]], writes=["l1f_scr3"],
                        tag="l1fc2b")
                    def c3(e, d=dst[mt], DKi=DK[mt], cch=cch):
                        return e.tensor_tensor(d[:, cch * 512:cch * 512 + 512],
                                               scr3[:], banks[2][:], ALU.mult)
                    p.v(c3, reads=["l1f_scr3", banks[2]], writes=[DK[mt]],
                        tag="l1fc3")
        for tt in range(NT):
            def ms(e, vx=v_ext[tt]):
                v3 = vx.rearrange("p (h o) -> p h o", h=NHEADS)
                return e.memset(v3[:, :, 64:65], 1.0)
            p.v(ms, reads=[], writes=[VXK[tt]], tag="l1fvone")
        for kt in range(8):
            def dm(e, kt=kt, layer=layer):
                return e.dma_start(out=wslot[:, kt * 512:(kt + 1) * 512],
                                   in_=wqkv_d[layer, kt * 128:(kt + 1) * 128,
                                              1024:1536])
            p.op("gpsimd", dm, reads=[], writes=[wslot] + WSUBS, dma=True,
                 tag="l1f_ldv")
        for tt in range(NT):
            ps = banks[4 + tt % 2]
            for kt in range(8):
                def mm(e, ps=ps, tt=tt, kt=kt, st=(kt == 0), sp=(kt == 7)):
                    return e.matmul(
                        ps[:],
                        xbf[kt][:, tt * 128:tt * 128 + 128],
                        wslot[:, kt * 512:kt * 512 + 512],
                        start=st, stop=sp)
                p.t(mm, reads=["Wslot", XBK[kt]], writes=[ps], tag="l1fv")
            def ev(e, ps=ps, vx=v_ext[tt]):
                dst3 = vx.rearrange("p (h o) -> p h o", h=NHEADS)[:, :, 0:64]
                src3 = ps[:].rearrange("p (h o) -> p h o", h=NHEADS)
                return e.tensor_copy(dst3, src3)
            p.v(ev, reads=[ps], writes=[VXK[tt]], tag="l1fvev")

        if layer == 0:
            ctx.debug("l1fq0", q_bf[0][:], reads=[QK[0]])
            ctx.debug("l1fk0", k_bf[0][:], reads=[KK[0]])
            ctx.debug("l1fvx0", v_ext[0], reads=[VXK[0]])
        o_fm = xbf[0:4]
        OK_ = XBK[0:4]
        pts = [wslot[:, 0:512], wslot[:, 512:1024]]
        PTK = ["Wsub_pt0", "Wsub_pt1"]
        rrow = wslot[0:1, 1536:2560].bitcast(F32)
        rrow_bf = wslot[0:1, 2560:3072]
        rbc_bf = wslot[:, 1024:1536]
        for h in range(NHEADS):
            hp, ho = h // 2, (h % 2) * 64
            for w in range(4):
                po = banks[6]
                first_pv = [True]
                for ktl in range(8):
                    kta = (w - 1) * 4 + ktl
                    if kta < 0:
                        continue
                    qlo = max(0, (ktl - 4) * 128)
                    qhi = min(512, (ktl + 1) * 128)
                    if qlo >= qhi:
                        continue
                    width = qhi - qlo
                    ps = banks[4 + ktl % 2]
                    pt = pts[ktl % 2]
                    ptk = PTK[ktl % 2]
                    def mmS(e, ps=ps, hp=hp, ho=ho, kta=kta, w=w, qlo=qlo,
                            width=width):
                        return e.matmul(
                            ps[:, 0:width],
                            k_bf[hp][ho:ho + 64, kta * 128:kta * 128 + 128],
                            q_bf[hp][ho:ho + 64,
                                     w * 512 + qlo:w * 512 + qlo + width],
                            start=True, stop=True)
                    p.t(mmS, reads=[KK[hp], QK[hp]], writes=[ps], tag="l1fS")
                    def ex(e, ps=ps, pt=pt, qlo=qlo, width=width):
                        return e.activation(pt[:, qlo:qlo + width],
                                            ps[:, 0:width], AF.Exp,
                                            scale=8.0, bias=-60.0)
                    p.s(ex, reads=[ps], writes=[ptk], tag="l1fexp")
                    if ktl < 4:
                        tm, tq = triB, ktl * 128
                    else:
                        tm, tq = triA, (ktl - 4) * 128
                    def trm(e, pt=pt, tm=tm, tq=tq):
                        return e.tensor_tensor(pt[:, tq:tq + 128],
                                               pt[:, tq:tq + 128], tm[:],
                                               ALU.mult)
                    p.v(trm, reads=[ptk, tm], writes=[ptk], tag="l1ftri")
                    def mmO(e, po=po, pt=pt, kta=kta, h=h, qlo=qlo,
                            width=width, st=first_pv[0]):
                        return e.matmul(
                            po[0:65, qlo:qlo + width],
                            v_ext[kta][:, h * 65:h * 65 + 65],
                            pt[:, qlo:qlo + width], start=st, stop=False)
                    p.t(mmO, reads=[ptk, VXK[kta]], writes=[po], tag="l1fO")
                    first_pv[0] = False
                def rc(e, po=po):
                    return e.reciprocal(rrow, po[64:65, :])
                p.v(rc, reads=[po], writes=["Wsub_rrow"], tag="l1frec")
                def rcb(e):
                    return e.copy(rrow_bf, rrow)
                p.s(rcb, reads=["Wsub_rrow"], writes=["Wsub_rrowbf"],
                    tag="l1frecb")
                def bc(e):
                    return e.matmul(banks[5][:], onesrb[0:1, :], rrow_bf,
                                    start=True, stop=True)
                p.t(bc, reads=["Wsub_rrowbf", onesrb], writes=[banks[5]],
                    tag="l1frbc")
                def cpb(e):
                    return e.tensor_copy(rbc_bf, banks[5][:])
                p.v(cpb, reads=[banks[5]], writes=["Wsub_rbc"], tag="l1frbcc")
                def dv(e, po=po, d=o_fm[hp], ho=ho, w=w):
                    return e.tensor_tensor(d[ho:ho + 64, w * 512:w * 512 + 512],
                                           po[0:64, :], rbc_bf[ho:ho + 64, :],
                                           ALU.mult)
                p.v(dv, reads=[po, "Wsub_rbc"], writes=[OK_[hp]], tag="l1fdv")

        if layer == 0:
            ctx.debug("l1fo0", o_fm[0][:], reads=[OK_[0]])
        for kt in range(4):
            def dm(e, kt=kt, layer=layer):
                return e.dma_start(out=wslot[:, kt * 1024:(kt + 1) * 1024],
                                   in_=wo_d[layer, kt * 128:(kt + 1) * 128, :])
            p.op("gpsimd", dm, reads=[], writes=[wslot] + WSUBS, dma=True,
                 tag="l1f_ldwo")
        for mt in range(8):
            for cch in range(4):
                ps = banks[4 + (mt * 4 + cch) % 2]
                for kt in range(4):
                    def mm(e, ps=ps, kt=kt, mt=mt, cch=cch, st=(kt == 0),
                           sp=(kt == 3)):
                        return e.matmul(
                            ps[:],
                            wslot[:, kt * 1024 + mt * 128:
                                  kt * 1024 + mt * 128 + 128],
                            o_fm[kt][:, cch * 512:cch * 512 + 512],
                            start=st, stop=sp)
                    p.t(mm, reads=["Wslot", OK_[kt]], writes=[ps], tag="l1fwo")
                def acc(e, ps=ps, xt=x_tiles[mt], cch=cch):
                    return e.tensor_tensor(xt[:, cch * 512:cch * 512 + 512],
                                           xt[:, cch * 512:cch * 512 + 512],
                                           ps[:], ALU.add)
                p.v(acc, reads=[ps, x_tiles[mt]], writes=[x_tiles[mt]],
                    tag="l1fres")

        if layer == 0:
            ctx.debug("l1fxr0", x_tiles[0][:], reads=[x_tiles[0]])
        xbf = ctx.retile("B")
        sqt = ctx.retile("A")
        ln = emit_ln_stats_shift(ctx, x_tiles, xbf, sqt, f"l1fm{layer}")
        RK, negm_bf, r_bf, r_tm = ln["RK"], ln["negm_bf"], ln["r_bf"], ln["r_tm"]
        for i in range(8):
            def sc(e, b=xbf[i], i=i, layer=layer):
                return e.tensor_scalar(b[:], b[:],
                                       lng[:, layer * 8 + i:layer * 8 + i + 1],
                                       None, ALU.mult)
            p.v(sc, reads=[XBK[i], lng], writes=[XBK[i]], tag="l1flg")
        setA = ctx.retile("A")
        AOK = [f"A{i}" for i in range(8)]
        aout = [ctx.carve("setA", f"l1f_ao{i}", 0, 128, i * 6144, 6144, BF16,
                          free=[f"A{j}" for j in range(8)] if i == 0 else ())
                for i in range(4)]
        AOKc = [f"l1f_ao{i}" for i in range(4)]
        aotr = ctx.carve("setA", "l1f_aotr", 0, 128, 4 * 6144, 1024, BF16)
        for tch in range(4):
            t0 = tch * 512
            for lp in range(6):
                f0 = lp * 512
                for kt in range(8):
                    def dma_(e, kt=kt, layer=layer, f0=f0):
                        return e.dma_start(
                            out=wslot[:, kt * 512:(kt + 1) * 512],
                            in_=w1a_d[layer, kt * 128:(kt + 1) * 128,
                                      f0:f0 + 512])
                    p.op("gpsimd", dma_, reads=[], writes=[wslot] + WSUBS,
                         dma=True, tag="l1f_ldw1a")
                for kt in range(8):
                    def dma_(e, kt=kt, layer=layer, f0=f0):
                        return e.dma_start(
                            out=wslot[:, 4096 + kt * 512:4096 + (kt + 1) * 512],
                            in_=w1g_d[layer, kt * 128:(kt + 1) * 128,
                                      f0:f0 + 512])
                    p.op("gpsimd", dma_, reads=[], writes=[wslot] + WSUBS,
                         dma=True, tag="l1f_ldw1g")
                csA = rows_t[0:1, 1024:1280].bitcast(BF16)[0:1, 0:512]
                csG = rows_t[0:1, 1280:1536].bitcast(BF16)[0:1, 0:512]
                ps2 = banks[2][0:1, :]
                for (dstrow, off) in ((csA, 0), (csG, 4096)):
                    for kt in range(8):
                        def mm(e, kt=kt, off=off, layer=layer, ps2=ps2,
                               st=(kt == 0), sp=(kt == 7)):
                            return e.matmul(
                                ps2, lngb[:, layer * 8 + kt:layer * 8 + kt + 1],
                                wslot[:, off + kt * 512:off + kt * 512 + 512],
                                start=st, stop=sp)
                        p.t(mm, reads=["Wslot", lngb], writes=[banks[2]],
                            tag="l1fcs")
                    def evr(e, dstrow=dstrow, ps2=ps2):
                        return e.copy(dstrow, ps2)
                    p.s(evr, reads=[banks[2]], writes=["ln_rows"],
                        tag="l1fcsev")
                lbrow = ctx.pool("l1f_lbrow", [1, 1024], BF16)
                lbA = lbrow[0:1, 0:512]
                lbG = lbrow[0:1, 512:1024]
                for (dstrow, off) in ((lbA, 0), (lbG, 4096)):
                    for kt in range(8):
                        def mm(e, kt=kt, off=off, layer=layer, ps2=ps2,
                               st=(kt == 0), sp=(kt == 7)):
                            return e.matmul(
                                ps2, lnbb[:, layer * 8 + kt:layer * 8 + kt + 1],
                                wslot[:, off + kt * 512:off + kt * 512 + 512],
                                start=st, stop=sp)
                        p.t(mm, reads=["Wslot", lnbb], writes=[banks[2]],
                            tag="l1flb")
                    def evr(e, dstrow=dstrow, ps2=ps2):
                        return e.copy(dstrow, ps2)
                    p.s(evr, reads=[banks[2]], writes=["l1f_lbrow"],
                        tag="l1flbev")
                def dmb1(e, layer=layer, f0=f0):
                    return e.dma_start(out=b1row[0:1, 0:512],
                                       in_=b1a_d[layer, :, f0:f0 + 512])
                p.op("sync", dmb1, reads=[], writes=[b1row], dma=True,
                     tag="l1f_ldb1a")
                def dmb2(e, layer=layer, f0=f0):
                    return e.dma_start(out=b1row[0:1, 512:1024],
                                       in_=b1g_d[layer, :, f0:f0 + 512])
                p.op("sync", dmb2, reads=[], writes=[b1row], dma=True,
                     tag="l1f_ldb1g")
                def bcb(e):
                    return e.matmul(banks[3][:, 0:512], onesrb[0:1, :],
                                    b1row[0:1, 0:512], start=True, stop=False)
                p.t(bcb, reads=[b1row, onesrb], writes=[banks[3]], tag="l1fbb")
                def bcb2(e, lbA=lbA):
                    return e.matmul(banks[3][:, 0:512], onesrb[0:1, :],
                                    lbA, start=False, stop=True)
                p.t(bcb2, reads=["l1f_lbrow", onesrb], writes=[banks[3]],
                    tag="l1fbb2")
                def bcg(e):
                    return e.matmul(banks[6][:, 0:512], onesrb[0:1, :],
                                    b1row[0:1, 512:1024], start=True,
                                    stop=False)
                p.t(bcg, reads=[b1row, onesrb], writes=[banks[6]], tag="l1fbg")
                def bcg2(e, lbG=lbG):
                    return e.matmul(banks[6][:, 0:512], onesrb[0:1, :],
                                    lbG, start=False, stop=True)
                p.t(bcg2, reads=["l1f_lbrow", onesrb], writes=[banks[6]],
                    tag="l1fbg2")
                biasA = qf32[:, 512:1024]
                biasG = qf32[:, 1024:1536]
                p.v(lambda e: e.tensor_copy(biasA, banks[3][:, 0:512]),
                    reads=[banks[3]], writes=["l1f_qf32"], tag="l1fbac")
                p.v(lambda e: e.tensor_copy(biasG, banks[6][:, 0:512]),
                    reads=[banks[6]], writes=["l1f_qf32"], tag="l1fbgc")
                for ttl in range(4):
                    tt = tch * 4 + ttl
                    psa, psg = banks[4], banks[5]
                    for (psx, off, csrow, lbrow_, b1off) in (
                            (psa, 0, csA, lbA, 0), (psg, 4096, csG, lbG, 512)):
                        for kt in range(8):
                            def mm(e, psx=psx, off=off, kt=kt, tt=tt,
                                   st=(kt == 0)):
                                return e.matmul(
                                    psx[:],
                                    xbf[kt][:, tt * 128:tt * 128 + 128],
                                    wslot[:, off + kt * 512:
                                          off + kt * 512 + 512],
                                    start=st, stop=False)
                            p.t(mm, reads=["Wslot", XBK[kt]], writes=[psx],
                                tag="l1fh1")
                        def mr1(e, psx=psx, csrow=csrow, tt=tt):
                            return e.matmul(
                                psx[:], negm_bf[0:1, tt * 128:tt * 128 + 128],
                                csrow, start=False, stop=False)
                        p.t(mr1, reads=[RK, "ln_rows"], writes=[psx],
                            tag="l1fr1")
                        def mr2(e, psx=psx, lbrow_=lbrow_):
                            return e.matmul(
                                psx[:], onesrb[0:1, :], lbrow_,
                                start=False, stop=True)
                        pass
                    rcol = r_tm[:, tt:tt + 1]
                    def gv(e, psg=psg, rcol=rcol):
                        return e.scalar_tensor_tensor(
                            scr3[:], psg[:], rcol, biasG,
                            ALU.mult, ALU.add)
                    p.v(gv, reads=[psg, r_tm, "l1f_qf32"],
                        writes=["l1f_scr3"], tag="l1fgv")
                    def gl(e):
                        return e.activation(scr3[:], scr3[:], AF.Gelu)
                    p.s(gl, reads=["l1f_scr3"], writes=["l1f_scr3"],
                        tag="l1fgelu")
                    def av(e, psa=psa, rcol=rcol):
                        return e.scalar_tensor_tensor(
                            qf32[:, 0:512], psa[:], rcol, biasA,
                            ALU.mult, ALU.add)
                    p.v(av, reads=[psa, r_tm, "l1f_qf32"],
                        writes=["l1f_qf32"], tag="l1fav")
                    def ao(e, ttl=ttl, lp=lp):
                        return e.tensor_tensor(
                            aout[ttl][:, lp * 512:lp * 512 + 512],
                            qf32[:, 0:512], scr3[:], ALU.mult)
                    p.v(ao, reads=["l1f_qf32", "l1f_scr3"],
                        writes=[AOKc[ttl]], tag="l1fao")
            if tch == 0:
                ctx.debug(f"l1fao{layer}", aout[0][:], reads=[AOKc[0]])
            for half in range(2):
                for ldw in range(3):
                    nk = 8
                    for kt in range(nk):
                        def dma_(e, kt=kt, ldw=ldw, layer=layer):
                            return e.dma_start(
                                out=wslot[:, kt * 1024:(kt + 1) * 1024],
                                in_=w2_d[layer, ldw * 1024 + kt * 128:
                                         ldw * 1024 + (kt + 1) * 128, :])
                        p.op("gpsimd", dma_, reads=[], writes=[wslot] + WSUBS,
                             dma=True, tag="l1f_ldw2")
                    for kt in range(nk):
                        fkt = ldw * 8 + kt
                        for ttl in range(4):
                            def tp(e, ttl=ttl, fkt=fkt):
                                return e.transpose(
                                    pstb[:, ttl * 128:ttl * 128 + 128],
                                    aout[ttl][:, fkt * 128:fkt * 128 + 128],
                                    identb[:])
                            p.t(tp, reads=[AOKc[ttl], identb],
                                writes=[banks[7]], tag="l1fat")
                        def ev(e):
                            return e.tensor_copy(aotr[:, 0:512],
                                                 pstb[:, 0:512])
                        p.v(ev, reads=[banks[7]], writes=["l1f_aotr"],
                            tag="l1fatev")
                        for mi in range(4):
                            mt = half * 4 + mi
                            psm = banks[mi]
                            def mm(e, psm=psm, kt=kt, mt=mt,
                                   st=(ldw == 0 and kt == 0)):
                                return e.matmul(
                                    psm[:],
                                    wslot[:, kt * 1024 + mt * 128:
                                          kt * 1024 + mt * 128 + 128],
                                    aotr[:, 0:512], start=st, stop=False)
                            p.t(mm, reads=["Wslot", "l1f_aotr"], writes=[psm],
                                tag="l1fw2")
                for mi in range(4):
                    mt = half * 4 + mi
                    psm = banks[mi]
                    def fin(e, psm=psm, mt=mt, t0=t0, layer=layer):
                        return e.scalar_tensor_tensor(
                            x_tiles[mt][:, t0:t0 + 512], psm[:],
                            b2c[:, layer * 8 + mt:layer * 8 + mt + 1],
                            x_tiles[mt][:, t0:t0 + 512], ALU.add, ALU.add)
                    p.v(fin, reads=[psm, b2c, x_tiles[mt]],
                        writes=[x_tiles[mt]], tag="l1ffin")
        ctx.debug(f"l1fml{layer}", x_tiles[0][:], reads=[x_tiles[0]])
    for i in range(8):
        def dma_(e, i=i):
            return e.dma_start(out=x_tiles[i][:], in_=spill2[i],
                               accum_op=ALU.add)
        p.op("gpsimd", dma_, reads=[spill2, x_tiles[i]], writes=[x_tiles[i]],
             dma=True, tag="l1f_unspill")
    ctx.retile("A")
    ctx.retile("B")
    ctx.retile("C")


def emit_ln_stats_shift(ctx, src_tiles, xbf, sq_tiles, tag):
    p, c = ctx.p, ctx.c
    for i in range(4, 8):
        pass
    return emit_ln_stats(ctx, src_tiles, xbf, sq_tiles, tag,
                         shift_half=True)



BLOCK_EMITTERS = {}


def _register_blocks():
    BLOCK_EMITTERS["l0a"] = emit_l0_attn
    BLOCK_EMITTERS["l0f"] = emit_l0_ffn
    BLOCK_EMITTERS["l1a"] = emit_l1_attn
    BLOCK_EMITTERS["l1f"] = emit_l1_ffn


def build_core_program(blocks, debug=False, l0f_stop=99):
    _register_blocks()
    nc = bass.Bass("TRN2", target_bir_lowering=False)
    spill = nc.dram_tensor("spill", [3, 128, T], F32, kind="Internal").ap()
    x_in = nc.dram_tensor("x_fm_in", [D, T], F32, kind="ExternalInput").ap()
    x_out = nc.dram_tensor("x_fm_out", [D, T], F32, kind="ExternalOutput").ap()
    stack = contextlib.ExitStack()
    with stack:
        prog = Prog(nc)
        lim = os.environ.get("KOPLIMIT")
        if lim:
            prog.limit = int(lim)
        gc_ = os.environ.get("KGROUPCAP")
        if gc_:
            prog.GROUP_CAP = int(gc_)
        ctx = Ctx(nc, prog, stack)
        ctx.c["spill_dram"] = spill
        ctx.c["_debug"] = debug
        ctx.c["_barrier"] = bool(int(os.environ.get("KBARRIER", "0")))
        ctx.c["_l0f_stop"] = l0f_stop
        sems = {e: stack.enter_context(nc.semaphore(f"sem_{e}"))
                for e in ENGINES}
        qsems = {q: stack.enter_context(nc.semaphore(q))
                 for q in DMA_QUEUES.values()}
        emit_setup_consts(ctx)
        ctx.arena_init("arena", 33280)
        ctx.arena_init("setA", 32768)
        ctx.arena_init("setB", 32768)
        ctx.arena_init("setC", 32768)
        ctx.arena_init("bscr", 12032)
        x_tiles = [ctx.carve("arena", f"x{i}", 0, 128, i * 8192, 8192, F32)
                   for i in range(3)]
        x_tiles += [ctx.sb(f"x{i}", [128, T], F32) for i in range(3, 8)]
        ctx.c["x_tiles"] = x_tiles
        for i in range(8):
            prog.dma("sync", x_tiles[i][:], x_in[i * 128:(i + 1) * 128, :],
                     tag="ld_x")
        for b in blocks:
            ctx.bpool_reset()
            if ctx.c.get("_barrier"):
                prog.barrier()
            BLOCK_EMITTERS[b](ctx, x_tiles)
        for i in range(8):
            prog.dma("sync", x_out[i * 128:(i + 1) * 128, :], x_tiles[i][:],
                     tag="st_x")
        prog.op("sync", lambda e: e.nop(), reads=[x_out], writes=[],
                tag="flush")
        if os.environ.get("KPOSTDUMP"):
            prog.barrier(tag="pdbar")
            dumps = [("arena", BF16), ("setA", BF16), ("setB", BF16),
                     ("setC", BF16), ("bscr", BF16)]
            for nm, dt_ in dumps:
                t = ctx.c["_arenas"][nm]["t"]
                d = nc.dram_tensor(f"pd_{nm}", list(t.shape), dt_,
                                   kind="ExternalOutput").ap()
                prog.dma("sync", d, t[:], tag=f"pd_{nm}")
            for nm in ("Wslot", "ln_rows"):
                if nm in ctx.c:
                    t = ctx.c[nm]
                    d = nc.dram_tensor(f"pd_{nm}", list(t.shape), t.dtype,
                                       kind="ExternalOutput").ap()
                    prog.dma("sync", d, t[:], tag=f"pd_{nm}")
            for i in range(3, 8):
                d = nc.dram_tensor(f"pd_x{i}", [128, T], F32,
                                   kind="ExternalOutput").ap()
                prog.dma("sync", d, x_tiles[i][:], tag=f"pd_x{i}")
            prog.op("sync", lambda e: e.nop(),
                    reads=[f"pd_x{i}" for i in range(3, 8)], writes=[],
                    tag="flush")
        with nc.Block() as block:
            prog.emit_all(block, sems, qsems)
    return nc, prog


def host_weights_l0a(inputs):
    return {
        "L0_WqT": np.ascontiguousarray(inputs["L0_attn_Wq"].T),
        "L0_WkT": np.ascontiguousarray(inputs["L0_attn_Wk"].T),
        "L0_WvT": np.ascontiguousarray(inputs["L0_attn_Wv"].T),
        "L0_WoT": np.ascontiguousarray(inputs["L0_attn_Wo"].T),
    }


def host_weights_l0f(inputs):
    f1 = np.asarray(inputs["L0_ffn_f1"], np.float32)
    f2 = np.asarray(inputs["L0_ffn_f2"], np.float32)
    bias = np.asarray(inputs["L0_ffn_bias"], np.float32)
    ctrl = np.asarray(inputs["L0_ffn_controller"], np.float32)
    a = f1.reshape(1024, 64, 64)
    f1p = np.ascontiguousarray(
        a.reshape(8, 128, 4, 16, 64).transpose(2, 0, 1, 3, 4)
        .reshape(4, 8, 128, 16 * 64))
    f2p = np.ascontiguousarray(f2.reshape(64, 64, 1024))
    biasT = np.ascontiguousarray(bias.reshape(64, 64).T)
    bf = ml_dtypes.bfloat16
    tl = np.zeros(128, np.int64)
    tie = np.tile(np.linspace(0.0, 1e-6, 16, dtype=np.float32), 128)
    glm = np.zeros((128, 128), np.float32)
    for t in range(128):
        for e in range(16):
            glm[t, e * 8 + t // 16] = 1.0
    return {
        "L0F_ctrl": np.ascontiguousarray(ctrl.reshape(1024, 64)),
        "L0F_f1": f1p,
        "L0F_f2": f2p,
        "L0F_biasT": biasT,
        "hc_tie": tie[None].astype(bf),
        "hc_glmask": glm.astype(bf),
    }


def host_weights_l1a(inputs):
    W = np.asarray(inputs["L1_attn_in_proj_w"], np.float32)
    Wo = np.asarray(inputs["L1_attn_out_proj_w"], np.float32)
    cw = np.asarray(inputs["L1_attn_conv_w"], np.float32)[:, 0, :]
    cb = np.asarray(inputs["L1_attn_conv_b"], np.float32)
    convw = np.zeros((128, 72), np.float32)
    convb = np.zeros((128, 18), np.float32)
    for g in range(16):
        convw[:, g * 4:(g + 1) * 4] = cw[g * 128:(g + 1) * 128]
        convb[:, g] = cb[g * 128:(g + 1) * 128]
    convw[0:64, 64:68] = cw[2048:2112]
    convw[0:64, 68:72] = cw[2112:2176]
    convb[0:64, 16] = cb[2048:2112]
    convb[0:64, 17] = cb[2112:2176]
    i_ = np.arange(128)[:, None]
    j_ = np.arange(256)[None, :]
    tri0 = (j_ >= i_).astype(np.float32)
    tri1 = (j_ >= i_ + 128).astype(np.float32)
    return {
        "L1A_WT": np.ascontiguousarray(W.T),
        "L1A_WoT": np.ascontiguousarray(Wo.T),
        "L1A_convw": convw,
        "L1A_convb": convb,
        "L1A_dtb": np.ascontiguousarray(
            np.asarray(inputs["L1_attn_dt_bias"], np.float32)[:, None]),
        "L1A_Alog": np.ascontiguousarray(
            np.asarray(inputs["L1_attn_A_log"], np.float32)[:, None]),
        "L1A_D": np.ascontiguousarray(
            np.asarray(inputs["L1_attn_D_param"], np.float32)[None, :]),
        "L1A_normw": np.ascontiguousarray(
            np.asarray(inputs["L1_attn_norm_w"], np.float32)
            .reshape(16, 128).T),
        "hc_tri0": tri0.astype(ml_dtypes.bfloat16),
        "hc_tri1": tri1.astype(ml_dtypes.bfloat16),
    }


def host_weights_l1f(inputs):
    bf = ml_dtypes.bfloat16
    f32 = np.float32
    wqkv = np.asarray(inputs["L1_ffn_wqkv"], f32)
    wo = np.asarray(inputs["L1_ffn_wo"], f32)
    w1 = np.asarray(inputs["L1_ffn_w1"], f32)
    w2 = np.asarray(inputs["L1_ffn_w2"], f32)
    b1 = np.asarray(inputs["L1_ffn_b1"], f32)
    b2 = np.asarray(inputs["L1_ffn_b2"], f32)
    lg = np.asarray(inputs["L1_ffn_ln_g"], f32)
    lb = np.asarray(inputs["L1_ffn_ln_b"], f32)
    qs = np.asarray(inputs["L1_ffn_q_scale"], f32)
    ks = np.asarray(inputs["L1_ffn_k_scale"], f32)
    HID = 2730
    w1a = np.zeros((2, 1024, 3072), f32)
    w1g = np.zeros((2, 1024, 3072), f32)
    w1a[:, :, :HID] = w1[:, :, :HID]
    w1g[:, :, :HID] = w1[:, :, HID:]
    w2p = np.zeros((2, 3072, 1024), f32)
    w2p[:, :HID] = w2
    b1a = np.zeros((2, 1, 3072), f32)
    b1g = np.zeros((2, 1, 3072), f32)
    b1a[:, 0, :HID] = b1[:, :HID]
    b1g[:, 0, :HID] = b1[:, HID:]
    rotsrc = np.array([(d + 32) % 64 for d in range(64)])
    qsc = np.zeros((128, 8), f32)
    for l in range(2):
        qsc[:, l * 4 + 0] = np.tile(qs[l], 2)
        qsc[:, l * 4 + 1] = np.tile(qs[l][rotsrc], 2)
        qsc[:, l * 4 + 2] = np.tile(ks[l], 2)
        qsc[:, l * 4 + 3] = np.tile(ks[l][rotsrc], 2)
    lng = np.zeros((128, 16), f32)
    lnb = np.zeros((128, 16), f32)
    b2c = np.zeros((128, 16), f32)
    for l in range(2):
        lng[:, l * 8:(l + 1) * 8] = lg[l].reshape(8, 128).T
        lnb[:, l * 8:(l + 1) * 8] = lb[l].reshape(8, 128).T
        b2c[:, l * 8:(l + 1) * 8] = b2[l].reshape(8, 128).T
    DH, WIN, n = 64, 512, T
    half = np.arange(0, DH, 2, dtype=f32)
    inv_freq = 1.0 / 10000.0 ** (half / DH)
    t_ = np.arange(n, dtype=f32)
    freqs = t_[:, None] * inv_freq[None, :]
    freqs = np.concatenate([freqs, freqs], axis=-1)
    base = (half + 0.4 * DH) / (1.4 * DH)
    power = (t_ - n // 2) / (WIN // 2)
    scale = base[None, :] ** power[:, None]
    scale = np.concatenate([scale, scale], axis=-1)
    cos, sin = np.cos(freqs), np.sin(freqs)
    def fm(tab):
        return np.ascontiguousarray(np.tile(tab.T, (2, 1)))
    rotp = np.zeros((128, 128), f32)
    for blk in range(2):
        for d in range(64):
            e_, s_ = ((d + 32), -1.0) if d < 32 else ((d - 32), 1.0)
            rotp[blk * 64 + e_, blk * 64 + d] = s_
    i_ = np.arange(128)
    triA = (i_[None, :] >= i_[:, None]).astype(f32)
    triB = (i_[None, :] < i_[:, None]).astype(f32)
    bo64 = np.zeros((128, 2), f32)
    bo64[0:64, 0] = 1.0
    bo64[64:128, 1] = 1.0
    repl2 = np.zeros((2, 128), f32)
    repl2[0, 0:64] = 1.0
    repl2[1, 64:128] = 1.0
    return {
        "L1F_wqkv": wqkv, "L1F_wo": wo, "L1F_w1a": w1a, "L1F_w1g": w1g,
        "L1F_w2": w2p, "L1F_qs": qsc, "L1F_lng": lng, "L1F_lnb": lnb,
        "L1F_lnbb": lnb.astype(bf), "L1F_lngb": lng.astype(bf),
        "L1F_b1a": b1a.astype(bf), "L1F_b1g": b1g.astype(bf), "L1F_b2": b2c,
        "hc_rotp": rotp.astype(bf),
        "hc_triA": triA.astype(bf), "hc_triB": triB.astype(bf),
        "hc_csq": fm(cos * scale).astype(bf), "hc_ssq": fm(sin * scale).astype(bf),
        "hc_csk": fm(cos / scale).astype(bf), "hc_ssk": fm(sin / scale).astype(bf),
        "hc_bo64": bo64.astype(bf), "hc_repl2": repl2.astype(bf),
    }


HOST_WEIGHTS = {"l0a": host_weights_l0a, "l0f": host_weights_l0f,
                "l1a": host_weights_l1a, "l1f": host_weights_l1f}


def kernel(**inputs):
    x = np.asarray(inputs["L0_attn_x"], np.float32)
    stages = [("l0a", "l0f", "l1a", "l1f")]
    cur = [np.ascontiguousarray(x[b].T) for b in range(N_CORES)]
    consts = host_consts()
    for blocks in stages:
        nc, _ = build_core_program(blocks)
        wmap = dict(consts)
        for b in blocks:
            wmap.update(HOST_WEIGHTS[b](inputs))
        in_maps = []
        for core in range(N_CORES):
            m = dict(wmap)
            m["x_fm_in"] = cur[core]
            in_maps.append(m)
        res = run_bass_kernel_spmd(nc, in_maps, core_ids=list(range(N_CORES)))
        cur = [res.results[c]["x_fm_out"] for c in range(N_CORES)]
    out = np.stack([np.asarray(c, np.float32).T for c in cur])
    return np.ascontiguousarray(out)
